# version 12
# baseline (speedup 1.0000x reference)
import math
from contextlib import ExitStack

import numpy as np
import ml_dtypes

import concourse.bass as bass
import concourse.mybir as mybir
from concourse.bass_utils import run_bass_kernel_spmd

F32 = mybir.dt.float32
BF16 = mybir.dt.bfloat16
AF = mybir.ActivationFunctionType
ALU = mybir.AluOpType
AX = mybir.AxisListType
BFNP = ml_dtypes.bfloat16

S = 4096
D = 2048
QKV_STORES = "all"
CONV_PARTS = "all"
LN_DBG = [None]
NH = 16
FI = 8192
NT = S // 128
SC_ATT = 1.0 / math.sqrt(128.0)
ESHIFT = -12.0
EPS = 1e-5


class Sem:
    def __init__(self, nc, es, name):
        self.h = es.enter_context(nc.semaphore(name))
        self.n = 0
        self.base = 0

    def inc(self, instr, k=1):
        instr.then_inc(self.h, k)
        self.n += k
        return instr

    def w(self, eng, v):
        eng.wait_ge(self.h, self.base + v)


def build_nc(upto=99, tap=None, dbg=False, order_names=None):
    nc = bass.Bass("TRN2", target_bir_lowering=False)
    dt = nc.dram_tensor

    x_in = dt("x", [S, D], F32, kind="ExternalInput").ap()
    w_att = {}
    for fl in ("a", "b"):
        for w in ("q", "k", "v", "o"):
            w_att[w + fl] = dt(f"w{w}{fl}", [D, D], F32, kind="ExternalInput").ap()
    b_att = {w: dt(f"b{w}", [D], F32, kind="ExternalInput").ap()
             for w in ("q", "k", "v", "o")}
    wg_in = dt("wg", [D, FI], F32, kind="ExternalInput").ap()
    wu_in = dt("wu", [D, FI], F32, kind="ExternalInput").ap()
    wd_in = dt("wd", [FI, D], F32, kind="ExternalInput").ap()
    y_out = dt("y", [S, D], F32, kind="ExternalOutput").ap()

    xres = dt("xres", [S, D], F32).ap()
    xbf = dt("xbf", [S, D], BF16).ap()
    xnfm = dt("xnfm", [16, 128, S], BF16).ap()
    wc = {k: dt(f"wc_{k}", [D, D], BF16).ap() for k in w_att}
    wt = {k: dt(f"wt_{k}", [D, D], BF16).ap()
          for k in ("qa", "ka", "va", "qb", "kb", "vb")}
    qs = dt("qs", [NH, 128, S], BF16).ap()
    ks = dt("ks", [NH, 128, S], BF16).ap()
    vs = dt("vs", [NH, NT, 128, 128], BF16).ap()
    zs = dt("zs", [NH, 128, S], BF16).ap()
    wdp = dt("wdp", [64, 128, D], BF16).ap()
    hscr = dt("hscr", [64, 128, S], BF16).ap()

    ges = ExitStack()
    block = ges.enter_context(nc.Block())

    semcache = {}

    def mksem(name):
        s = semcache.get(name)
        if s is None:
            s = Sem(nc, ges, name)
            semcache[name] = s
        s.base = s.n
        return s

    prev = [[]]
    uidc = [0]

    def join(eng):
        for s, v in prev[0]:
            if v > 0:
                eng.wait_ge(s.h, v)

    ones_col = ges.enter_context(nc.sbuf_tensor("ones_col", [128, 8], BF16))
    ones_row = ges.enter_context(nc.sbuf_tensor("ones_row", [1, 512], BF16))
    ones_row_f = ges.enter_context(nc.sbuf_tensor("ones_row_f", [1, 128], F32))
    maskbig = ges.enter_context(nc.sbuf_tensor("maskbig", [128, 896], BF16))
    iota_s = ges.enter_context(nc.sbuf_tensor("iota_s", [128, 896], F32))
    brow = {w: ges.enter_context(nc.sbuf_tensor(f"brow_{w}", [1, D], BF16))
            for w in ("q", "k", "v", "o")}
    pb = [ges.enter_context(nc.psum_tensor(f"pb{i}", [128, 512], F32))
          for i in range(8)]
    cconst = ges.enter_context(nc.sbuf_tensor("cconst", [128, 4], F32))

    g_const = mksem("g_const")
    g_conv = mksem("g_conv")

    @block.gpsimd
    def _(g):
        g.memset(ones_col[:], 1.0)
        g.memset(cconst[:, 0:1], EPS)
        g.memset(cconst[:, 1:2], ESHIFT)
        g.memset(ones_row[:], 1.0)
        g.memset(ones_row_f[:], 1.0)
        g_const.inc(g.iota(iota_s[:], pattern=[[1, 896]], base=0,
                           channel_multiplier=-1,
                           allow_small_or_imprecise_dtypes=True), 1)

    def emit_conv(part="all"):
        @block.gpsimd
        def _(g):
            if CONV_PARTS == "none":
                return
            if part in ("a", "all"):
                for k in ("qa", "ka", "va", "oa"):
                    g_conv.inc(g.dma_start(out=wc[k], in_=w_att[k]), 16)
            if part == "a":
                return
            for k in ("qb", "kb", "vb", "ob"):
                g_conv.inc(g.dma_start(out=wc[k], in_=w_att[k]), 16)
            if CONV_PARTS == "wc":
                return
            for w in ("q", "k", "v", "o"):
                g_conv.inc(g.dma_start(
                    out=brow[w][:],
                    in_=b_att[w].rearrange("(a n) -> a n", a=1)), 16)
            if CONV_PARTS == "wcbias":
                return
            for fc in range(64):
                w_, c_ = fc // 4, fc % 4
                g_conv.inc(g.dma_start(
                    out=wdp[fc],
                    in_=wd_in[w_ * 512 + c_: w_ * 512 + c_ + 509: 4, :]), 16)

    @block.vector
    def _(v):
        v.wait_ge(g_const.h, 1)
        g_const.inc(v.tensor_scalar(maskbig[:], iota_s[:], 384.0, None,
                                    ALU.is_ge), 1)

    CONV_A = 4 * 16
    CONV_B = 8 * 16
    CONV_BIAS = 12 * 16
    CONV_WDP = CONV_BIAS + 64 * 16
    nphase = [0]

    def done(finals):
        prev[0] = [(s, s.n) for s in finals]
        nphase[0] += 1
        return nphase[0] >= upto

    def emit_ln(src_ap, piggy=None, mid_pool=None, post_pool=None):
        es = ExitStack()
        uidc[0] += 1
        _u = uidc[0]

        def sb(nm_, shape_, dt_):
            return nc.sbuf_tensor(f"{nm_}_{_u}", shape_, dt_)
        xfm = es.enter_context(sb("ln_xf", [128, 16 * 2048], BF16))
        sqf = es.enter_context(sb("ln_sq", [128, 16 * 1024], BF16))
        rows = es.enter_context(sb("ln_rows", [1, 3 * 1024], F32))
        rowb = es.enter_context(sb("ln_rowb", [1, 2 * 1024], BF16))
        bcst = es.enter_context(sb("ln_bc", [128, 2 * 1024], F32))
        xnd = [es.enter_context(sb(f"ln_xd{i}", [128, 1024], BF16)) for i in range(2)]
        xng = [es.enter_context(sb(f"ln_xg{i}", [128, 1024], BF16)) for i in range(2)]
        t1f = es.enter_context(sb("ln_t1", [128, 1024], F32))
        t1g = es.enter_context(sb("ln_t2", [128, 1024], F32))
        d_c = mksem("ln_dc")
        d_x = mksem("ln_dx")
        d_o = mksem("ln_do")
        a_q = mksem("ln_aq")
        a_r = mksem("ln_arow")
        a_e = mksem("ln_ae")
        peS = mksem("ln_pes")
        peB = mksem("ln_peb")
        d_m = mksem("ln_dm")
        d_n = mksem("ln_dn")
        d_g = mksem("ln_dg")
        NB = 4

        def xsl(b, dk):
            return xfm[:, dk * 2048 + (b % 2) * 1024:
                       dk * 2048 + (b % 2) * 1024 + 1024]

        def emit_evac(sc, b):
            peB.w(sc, b + 1)
            if b >= 1:
                d_n.w(sc, 8 * b)
                d_g.w(sc, 8 * b)
            sc.activation(bcst[:, 0:512], pb[4][:], AF.Copy)
            sc.activation(bcst[:, 512:1024], pb[5][:], AF.Copy)
            sc.activation(bcst[:, 1024:1536], pb[6][:], AF.Copy)
            mi = sc.activation(bcst[:, 1536:2048], pb[7][:], AF.Copy)
            a_e.inc(mi)

        @block.gpsimd
        def _(g):
            join(g)
            for c in range(NB):
                d_c.inc(g.dma_start(out=xbf[c * 1024:(c + 1) * 1024, :],
                                    in_=src_ap[c * 1024:(c + 1) * 1024, :]),
                        16)

        if mid_pool is not None:
            mid_pool()

        @block.gpsimd
        def _(g):
            for b in range(NB):
                a_e.w(g, b + 1)
                for dk in range(8, 16):
                    if b * 16 + dk >= 2:
                        d_o.w(g, 16 * (16 * b + dk - 1))
                    g.tensor_tensor(t1g[:], xsl(b, dk), bcst[:, 0:1024],
                                    ALU.add)
                    d_g.inc(g.tensor_tensor(xng[dk % 2][:], t1g[:],
                                            bcst[:, 1024:2048], ALU.mult))

        if post_pool is not None:
            post_pool()

        def emit_stores(sy, b):
            for dk in range(16):
                if dk < 8:
                    d_n.w(sy, 8 * b + dk + 1)
                    srcb = xnd[dk % 2]
                else:
                    d_g.w(sy, 8 * b + (dk - 8) + 1)
                    srcb = xng[dk % 2]
                d_o.inc(sy.dma_start(
                    out=xnfm[dk, :, b * 1024:(b + 1) * 1024],
                    in_=srcb[:]), 16)

        @block.sync
        def _(sy):
            join(sy)
            for sbk in range(2):
                d_c.w(sy, 32 * (sbk + 1))
                if sbk > 0:
                    d_n.w(sy, 16)
                    d_g.w(sy, 16)
                for dk in range(16):
                    d_x.inc(sy.dma_start_transpose(
                        out=xfm[:, dk * 2048:(dk + 1) * 2048],
                        in_=xbf[sbk * 2048:(sbk + 1) * 2048,
                                dk * 128:(dk + 1) * 128]), 16)
                for bb in range(2):
                    emit_stores(sy, sbk * 2 + bb)

        @block.scalar
        def _(sc):
            join(sc)
            sc.wait_ge(g_const.h, 1)
            for b in range(NB):
                if b >= 1:
                    emit_evac(sc, b - 1)
                d_x.w(sc, 256 * (b // 2 + 1))
                if b >= 1:
                    peS.w(sc, b)
                for dk in range(16):
                    mi = sc.activation(sqf[:, dk * 1024:(dk + 1) * 1024],
                                       xsl(b, dk), AF.Square)
                a_q.inc(mi)
                d_m.w(sc, 2 * b + 1)
                sc.activation(rows[0:1, 1024:2048], rows[0:1, 2048:3072],
                              AF.Ln, bias=cconst[0:1, 0:1])
                if b >= 1:
                    peB.w(sc, b)
                mi = sc.activation(rowb[0:1, 1024:2048], rows[0:1, 1024:2048],
                                   AF.Exp, scale=-0.5)
                a_r.inc(mi)
                if piggy is not None:
                    piggy(sc, b)
            emit_evac(sc, NB - 1)

        @block.tensor
        def _(te):
            join(te)
            te.wait_ge(g_const.h, 1)
            for b in range(NB):
                d_x.w(te, 256 * (b // 2 + 1))
                if b >= 1:
                    d_m.w(te, 2 * b)
                for half in range(2):
                    for dk in range(16):
                        mi = te.matmul(pb[0 + half][0:1, :], ones_col[:, 0:1],
                                       xsl(b, dk)[:, half * 512:
                                                  half * 512 + 512],
                                       start=(dk == 0), stop=(dk == 15))
                a_q.w(te, b + 1)
                for half in range(2):
                    for dk in range(16):
                        mi = te.matmul(pb[2 + half][0:1, :], ones_col[:, 0:1],
                                       sqf[:, dk * 1024 + half * 512:
                                           dk * 1024 + half * 512 + 512],
                                       start=(dk == 0), stop=(dk == 15))
                peS.inc(mi)
                d_m.w(te, 2 * b + 2)
                if b >= 1:
                    a_e.w(te, b)
                te.matmul(pb[4][:], ones_row[0:1, 0:128],
                          rowb[0:1, 0:512], start=True, stop=True)
                te.matmul(pb[5][:], ones_row[0:1, 0:128],
                          rowb[0:1, 512:1024], start=True, stop=True)
                a_r.w(te, b + 1)
                te.matmul(pb[6][:], ones_row[0:1, 0:128],
                          rowb[0:1, 1024:1536], start=True, stop=True)
                mi = te.matmul(pb[7][:], ones_row[0:1, 0:128],
                               rowb[0:1, 1536:2048], start=True, stop=True)
                peB.inc(mi)

        @block.vector
        def _(v):
            join(v)
            for b in range(NB):
                peS.w(v, b + 1)
                v.tensor_scalar(rows[0:1, 0:512], pb[0][0:1, :], -1.0 / D,
                                None, ALU.mult)
                v.tensor_scalar(rows[0:1, 512:1024], pb[1][0:1, :], -1.0 / D,
                                None, ALU.mult)
                v.tensor_tensor(rows[0:1, 1024:2048], rows[0:1, 0:1024],
                                rows[0:1, 0:1024], ALU.mult)
                v.tensor_scalar(rows[0:1, 2048:2560], pb[2][0:1, :], 1.0 / D,
                                None, ALU.mult)
                v.tensor_scalar(rows[0:1, 2560:3072], pb[3][0:1, :], 1.0 / D,
                                None, ALU.mult)
                d_m.inc(v.tensor_tensor(rows[0:1, 2048:3072],
                                        rows[0:1, 2048:3072],
                                        rows[0:1, 1024:2048], ALU.subtract))
                if b >= 1:
                    peB.w(v, b)
                d_m.inc(v.tensor_copy(rowb[0:1, 0:1024], rows[0:1, 0:1024]))
                a_e.w(v, b + 1)
                for dk in range(8):
                    if b * 16 + dk >= 2:
                        d_o.w(v, 16 * (16 * b + dk - 1))
                    v.tensor_tensor(t1f[:], xsl(b, dk), bcst[:, 0:1024],
                                    ALU.add)
                    d_n.inc(v.tensor_tensor(xnd[dk % 2][:], t1f[:],
                                            bcst[:, 1024:2048], ALU.mult))

        es.close()
        return [d_o, d_n, d_g, a_e, peB, d_c]

    def make_wtrans(fl):
        es = ExitStack()
        uidc[0] += 1
        _u = uidc[0]
        tb = [es.enter_context(nc.sbuf_tensor(f"wt_tb{i}_{_u}", [128, D], BF16))
              for i in range(2)]
        d_t = mksem("wt_dt")
        d_s = mksem("wt_ds")
        jobs = [(w, dk) for w in ("q", "k", "v") for dk in range(16)]

        def piggy(sy, slot):
            if slot == 0:
                sy.wait_ge(g_conv.h, CONV_A if fl == "a" else CONV_B)
            for j in range(slot * 12, slot * 12 + 12):
                w, dk = jobs[j]
                p = j % 2
                if j >= 2:
                    d_s.w(sy, 16 * (j - 1))
                d_t.inc(sy.dma_start_transpose(
                    out=tb[p][:],
                    in_=wc[w + fl][:, dk * 128:(dk + 1) * 128]), 16)
                d_t.w(sy, 16 * (j + 1))
                d_s.inc(sy.dma_start(
                    out=wt[w + fl][dk * 128:(dk + 1) * 128, :],
                    in_=tb[p][:]), 16)

        return piggy, [d_s], es

    def emit_qkv(fl):
        es = ExitStack()
        uidc[0] += 1
        _u = uidc[0]

        def sb(nm_, shape_, dt_):
            return nc.sbuf_tensor(f"{nm_}_{_u}", shape_, dt_)
        xnf = es.enter_context(sb("qk_xnf", [128, 16 * 2048], BF16))
        ws = {w: [es.enter_context(sb(f"qk_w{w}{i}", [128, 16 * 512], BF16))
                  for i in range(2)] for w in ("q", "k", "v")}
        qst = [es.enter_context(sb(f"qk_q{i}", [128, 4 * 512], BF16)) for i in range(2)]
        kst = [es.enter_context(sb(f"qk_k{i}", [128, 4 * 512], BF16)) for i in range(2)]
        vst = [es.enter_context(sb(f"qk_v{i}", [128, 4 * 512], BF16)) for i in range(2)]
        d_xn = mksem("qk_dxn")
        d_w = mksem("qk_dw")
        d_st = [mksem("qk_ds0"), mksem("qk_ds1")]
        peQ = mksem("qk_pe")
        acQ = mksem("qk_ac")
        bias = (fl == "b")
        steps = [(hf, hb, t4) for hf in range(2) for hb in range(4)
                 for t4 in range(4)]

        @block.sync
        def _(sy):
            join(sy)
            for si, (hf, hb, t4) in enumerate(steps):
                p = si % 2
                if hb == 0 and t4 == 0:
                    if hf > 0:
                        peQ.w(sy, 12 * si)
                    for dk in range(16):
                        d_xn.inc(sy.dma_start(
                            out=xnf[:, dk * 2048:(dk + 1) * 2048],
                            in_=xnfm[dk, :, hf * 2048:(hf + 1) * 2048]), 16)
                if t4 == 0:
                    gw = si // 4
                    if gw >= 2:
                        peQ.w(sy, 48 * (gw - 1))
                    for w in ("q", "k", "v"):
                        d_w.inc(sy.dma_start(
                            out=ws[w][gw % 2][:]
                                .rearrange("p (dk o) -> p dk o", dk=16),
                            in_=wt[w + fl].rearrange("(dk p) o -> p dk o",
                                                     p=128)
                            [:, :, hb * 512:(hb + 1) * 512]), 16)
                h0 = hb * 4
                tg = hf * 4 + t4
                acQ.w(sy, 3 * si + 1)
                if QKV_STORES in ("all", "qk"):
                    d_st[p].inc(sy.dma_start(
                        out=qs[h0:h0 + 4, :, tg * 512:(tg + 1) * 512]
                            .rearrange("h p t -> p h t"),
                        in_=qst[p][:].rearrange("p (h t) -> p h t", h=4)), 16)
                acQ.w(sy, 3 * si + 2)
                if QKV_STORES in ("all", "qk"):
                    d_st[p].inc(sy.dma_start(
                        out=ks[h0:h0 + 4, :, tg * 512:(tg + 1) * 512]
                            .rearrange("h p t -> p h t"),
                        in_=kst[p][:].rearrange("p (h t) -> p h t", h=4)), 16)
                acQ.w(sy, 3 * si + 3)
                if QKV_STORES in ("all", "v"):
                    for ts in range(4):
                        d_st[p].inc(sy.dma_start(
                            out=vs[h0:h0 + 4, tg * 4 + ts, :, :]
                                .rearrange("h t dh -> t h dh"),
                            in_=vst[p][:, ts * 512:(ts + 1) * 512]
                                .rearrange("t (h dh) -> t h dh", h=4)), 16)
            if dbg:
                peQ.w(sy, 12 * 32)
                d_st[0].inc(sy.dma_start(out=dbg_xnf, in_=xnf[:]), 16)
                d_st[0].inc(sy.dma_start(out=dbg_wq, in_=ws["q"][:]), 16)
                d_st[0].inc(sy.dma_start(out=dbg_qst, in_=qst[1][:]), 16)

        @block.tensor
        def _(te):
            join(te)
            te.wait_ge(g_const.h, 1)
            if bias:
                te.wait_ge(g_conv.h, CONV_BIAS)
            for si, (hf, hb, t4) in enumerate(steps):
                h0 = hb * 4
                gwp = (si // 4) % 2
                d_xn.w(te, 256 * (hf + 1))
                d_w.w(te, 48 * (si // 4 + 1))
                if si > 0:
                    acQ.w(te, 3 * si)
                for h in range(4):
                    for dk in range(16):
                        mi = te.matmul(pb[h][:],
                                       ws["q"][gwp][:, dk * 512 + h * 128:
                                               dk * 512 + (h + 1) * 128],
                                       xnf[:, dk * 2048 + t4 * 512:
                                           dk * 2048 + (t4 + 1) * 512],
                                       start=(dk == 0),
                                       stop=(dk == 15 and not bias))
                    if bias:
                        mi = te.matmul(pb[h][:],
                                       brow["q"][0:1, (h0 + h) * 128:
                                                 (h0 + h + 1) * 128],
                                       ones_row[0:1, :],
                                       start=False, stop=True)
                    peQ.inc(mi)
                for h in range(4):
                    for dk in range(16):
                        mi = te.matmul(pb[4 + h][:],
                                       ws["k"][gwp][:, dk * 512 + h * 128:
                                               dk * 512 + (h + 1) * 128],
                                       xnf[:, dk * 2048 + t4 * 512:
                                           dk * 2048 + (t4 + 1) * 512],
                                       start=(dk == 0),
                                       stop=(dk == 15 and not bias))
                    if bias:
                        mi = te.matmul(pb[4 + h][:],
                                       brow["k"][0:1, (h0 + h) * 128:
                                                 (h0 + h + 1) * 128],
                                       ones_row[0:1, :],
                                       start=False, stop=True)
                    peQ.inc(mi)
                acQ.w(te, 3 * si + 1)
                for ts in range(4):
                    for dk in range(16):
                        mi = te.matmul(pb[ts][:],
                                       xnf[:, dk * 2048 + t4 * 512 + ts * 128:
                                           dk * 2048 + t4 * 512 + (ts + 1) * 128],
                                       ws["v"][gwp][:, dk * 512:(dk + 1) * 512],
                                       start=(dk == 0),
                                       stop=(dk == 15 and not bias))
                    if bias:
                        mi = te.matmul(pb[ts][:],
                                       ones_row[0:1, 0:128],
                                       brow["v"][0:1, h0 * 128:h0 * 128 + 512],
                                       start=False, stop=True)
                    peQ.inc(mi)

        @block.scalar
        def _(sc):
            join(sc)
            for si in range(32):
                p = si % 2
                if si >= 2:
                    nst = {"all": 6, "qk": 2, "v": 4, "none": 0}[QKV_STORES]
                    if nst:
                        d_st[p].w(sc, 16 * nst * (si // 2))
                peQ.w(sc, 12 * si + 4)
                for h in range(4):
                    mi = sc.activation(qst[p][:, h * 512:(h + 1) * 512],
                                       pb[h][:], AF.Copy)
                acQ.inc(mi)
                peQ.w(sc, 12 * si + 8)
                for h in range(4):
                    mi = sc.activation(kst[p][:, h * 512:(h + 1) * 512],
                                       pb[4 + h][:], AF.Copy)
                acQ.inc(mi)
                peQ.w(sc, 12 * si + 12)
                for ts in range(4):
                    mi = sc.activation(vst[p][:, ts * 512:(ts + 1) * 512],
                                       pb[ts][:], AF.Copy)
                acQ.inc(mi)

        es.close()
        return [d_st[0], d_st[1], peQ, acQ, d_xn, d_w]

    if dbg:
        dbg_xnf = dt("dbg_xnf", [128, 16 * 2048], BF16, kind="ExternalOutput").ap()
        dbg_wq = dt("dbg_wq", [128, 16 * 512], BF16, kind="ExternalOutput").ap()
        dbg_qst = dt("dbg_qst", [128, 4 * 512], BF16, kind="ExternalOutput").ap()

    def emit_attn(fl):
        es = ExitStack()
        uidc[0] += 1
        _u = uidc[0]

        def sb(nm_, shape_, dt_):
            return nc.sbuf_tensor(f"{nm_}_{_u}", shape_, dt_)
        Kb = [es.enter_context(sb(f"at_k{i}", [128, S], BF16)) for i in range(2)]
        Qb = [es.enter_context(sb(f"at_q{i}", [128, S], BF16)) for i in range(2)]
        Vb = [es.enter_context(sb(f"at_v{i}", [128, S], BF16)) for i in range(2)]
        PT = [es.enter_context(sb(f"at_pt{i}", [128, 512], BF16)) for i in range(6)]
        linv = [es.enter_context(sb(f"at_li{i}", [1, 512], F32)) for i in range(2)]
        lsc = es.enter_context(sb("at_lsc", [1, 512], F32))
        invb = [es.enter_context(sb(f"at_iv{i}", [128, 512], F32)) for i in range(2)]
        zst = [es.enter_context(sb(f"at_z{i}", [128, 512], BF16)) for i in range(2)]
        d_kv = [mksem("at_dk0"), mksem("at_dk1")]
        d_z = [mksem("at_dz0"), mksem("at_dz1")]
        peS = mksem("at_peS")
        peA = mksem("at_peA")
        peL = mksem("at_peL")
        peB = mksem("at_peB")
        peH = mksem("at_peH")
        acE = mksem("at_acE")
        acB = mksem("at_acB")
        dveM = mksem("at_dvM")
        dveR = mksem("at_dvR")
        dveZ = mksem("at_dvZ")

        def kbase(h, qb):
            return h * 144 + 2 * qb * qb + 2 * qb

        def emit_loads(sy, h):
            p = h % 2
            d_kv[p].inc(sy.dma_start(out=Kb[p][:], in_=ks[h]), 16)
            d_kv[p].inc(sy.dma_start(out=Qb[p][:], in_=qs[h]), 16)
            for piece in range(4):
                d_kv[p].inc(sy.dma_start(
                    out=Vb[p][:, piece * 1024:(piece + 1) * 1024]
                        .rearrange("t (tb dh) -> t tb dh", tb=8),
                    in_=vs[h, piece * 8:(piece + 1) * 8, :, :]
                        .rearrange("tb t dh -> t tb dh")), 16)

        @block.sync
        def _(sy):
            join(sy)
            emit_loads(sy, 0)
            emit_loads(sy, 1)
            for h in range(NH):
                for qb in range(8):
                    gz = h * 8 + qb
                    pq = gz % 2
                    dveZ.w(sy, gz + 1)
                    d_z[pq].inc(sy.dma_start(
                        out=zs[h][:, qb * 512:(qb + 1) * 512],
                        in_=zst[pq][:]), 16)
                if h + 2 < NH:
                    peH.w(sy, h + 1)
                    emit_loads(sy, h + 2)

        @block.tensor
        def _(te):
            join(te)
            te.wait_ge(g_const.h, 2)

            def pe_bcast(i):
                dveR.w(te, i + 1)
                if i >= 1:
                    acB.w(te, i)
                peB.inc(te.matmul(pb[6][:], ones_row_f[0:1, :],
                                  linv[i % 2][0:1, :], start=True, stop=True))

            for h in range(NH):
                p = h % 2
                d_kv[p].w(te, 96 * (h // 2 + 1))
                for qb in range(8):
                    gz = h * 8 + qb
                    kn = 4 * qb + 4
                    eb = kbase(h, qb)
                    if gz >= 2:
                        dveZ.w(te, gz - 1)
                        dveR.w(te, gz - 1)
                    bursts = []
                    s0 = 0
                    while s0 < kn:
                        bursts.append((s0, min(3, kn - s0)))
                        s0 += 3

                    def consume(c0, c1, kn=kn, qb=qb, gz=gz, eb=eb, p=p):
                        ne_hi = min(c1, 4 * qb - 1)
                        if ne_hi >= c0:
                            acE.w(te, eb + ne_hi + 1)
                        for c in range(c0, c1 + 1):
                            if c >= 4 * qb:
                                dveM.w(te, 4 * gz + (c - 4 * qb) + 1)
                            mi = te.matmul(pb[4 + gz % 2][0:1, :],
                                           ones_col[:, 0:1], PT[c % 6][:],
                                           start=(c == 0), stop=(c == kn - 1))
                            if c == kn - 1:
                                peL.inc(mi)
                            mi = te.matmul(pb[2 + gz % 2][:],
                                           Vb[p][:, c * 128:(c + 1) * 128],
                                           PT[c % 6][:],
                                           start=(c == 0),
                                           stop=(c == kn - 1))
                            if c % 2 == 1 or c == kn - 1:
                                peA.inc(mi)

                    for bi, (s0, sz) in enumerate(bursts):
                        for kb in range(s0, s0 + sz):
                            if kb >= 3:
                                acE.w(te, eb + kb - 2)
                            peS.inc(te.matmul(
                                pb[(0, 1, 7)[kb % 3]][:],
                                Kb[p][:, kb * 128:(kb + 1) * 128],
                                Qb[p][:, qb * 512:(qb + 1) * 512],
                                start=True, stop=True))
                        if gz >= 1 and (s0 == 6 or (kn == 4 and s0 == 3)):
                            pe_bcast(gz - 1)
                        if bi > 0:
                            ps0, psz = bursts[bi - 1]
                            consume(ps0, ps0 + psz - 1)
                    ls0, lsz = bursts[-1]
                    consume(ls0, ls0 + lsz - 1)
                peH.inc(te.nop())
            pe_bcast(NH * 8 - 1)

        @block.scalar
        def _(sc):
            join(sc)
            sc.wait_ge(g_const.h, 1)

            def ac_norm(i):
                peB.w(sc, i + 1)
                if i >= 2:
                    dveZ.w(sc, i - 1)
                acB.inc(sc.activation(invb[i % 2][:], pb[6][:], AF.Copy))

            for h in range(NH):
                for qb in range(8):
                    gz = h * 8 + qb
                    kn = 4 * qb + 4
                    eb = kbase(h, qb)
                    for c in range(kn):
                        peS.w(sc, eb + c + 1)
                        if c >= 6:
                            peA.w(sc, eb // 2 + (c - 6) // 2 + 1)
                        acE.inc(sc.activation(PT[c % 6][:],
                                              pb[(0, 1, 7)[c % 3]][:],
                                              AF.Exp, bias=cconst[:, 1:2],
                                              scale=SC_ATT))
                        if c == min(kn - 1, 7) - 2 and gz >= 1:
                            ac_norm(gz - 1)
            ac_norm(NH * 8 - 1)

        @block.vector
        def _(v):
            join(v)

            def dv_z(i):
                acB.w(v, i + 1)
                pq = i % 2
                if i >= 2:
                    d_z[pq].w(v, 16 * (i // 2))
                dveZ.inc(v.tensor_tensor(zst[pq][:], pb[2 + i % 2][:],
                                         invb[i % 2][:], ALU.mult))

            for h in range(NH):
                for qb in range(8):
                    gz = h * 8 + qb
                    kn = 4 * qb + 4
                    eb = kbase(h, qb)
                    for r in range(4):
                        c = 4 * qb + r
                        acE.w(v, eb + c + 1)
                        dveM.inc(v.tensor_tensor(
                            PT[c % 6][:], PT[c % 6][:],
                            maskbig[:, 384 - 128 * r: 896 - 128 * r],
                            ALU.mult))
                    if gz >= 1:
                        dv_z(gz - 1)
                    peL.w(v, gz + 1)
                    if gz >= 2:
                        peB.w(v, gz - 1)
                    dveR.inc(v.reciprocal(linv[gz % 2][:],
                                          pb[4 + gz % 2][0:1, :]))
            dv_z(NH * 8 - 1)

        es.close()
        return [d_z[0], d_z[1], peH, dveZ, acB, peB, d_kv[0], d_kv[1]]

    def emit_oproj(fl, res_src):
        es = ExitStack()
        uidc[0] += 1
        _u = uidc[0]

        def sb(nm_, shape_, dt_):
            return nc.sbuf_tensor(f"{nm_}_{_u}", shape_, dt_)
        wot = es.enter_context(sb("op_wot", [128, 16 * 2048], BF16))
        zbl = [es.enter_context(sb(f"op_z{i}", [128, 16 * 512], BF16)) for i in range(2)]
        xrb = [es.enter_context(sb(f"op_x{i}", [128, 4 * 2048], F32)) for i in range(2)]
        yst = [es.enter_context(sb(f"op_y{i}", [128, 2048], F32)) for i in range(2)]
        d_wo = mksem("op_dwo")
        d_z = [mksem("op_dz0"), mksem("op_dz1")]
        d_x = [mksem("op_dx0"), mksem("op_dx1")]
        d_y = mksem("op_dy")
        peY = mksem("op_pe")
        dveY = mksem("op_dv")
        bias = (fl == "b")

        @block.sync
        def _(sy):
            join(sy)
            sy.wait_ge(g_conv.h, CONV_A if fl == "a" else CONV_B)
            for dk in range(16):
                d_wo.inc(sy.dma_start_transpose(
                    out=wot[:, dk * 2048:(dk + 1) * 2048],
                    in_=wc["o" + fl][:, dk * 128:(dk + 1) * 128]), 16)
            for i in range(8):
                p = i % 2
                if i >= 2:
                    dveY.w(sy, 4 * (i - 1))
                d_z[p].inc(sy.dma_start(
                    out=zbl[p][:].rearrange("p (h t) -> p h t", h=16),
                    in_=zs[:, :, i * 512:(i + 1) * 512]
                        .rearrange("h p t -> p h t")), 16)
                d_x[p].inc(sy.dma_start(
                    out=xrb[p][:].rearrange("p (ts d2) -> p ts d2", ts=4),
                    in_=res_src[i * 512:(i + 1) * 512, :]
                        .rearrange("(ts p) d2 -> p ts d2", p=128)), 16)
                for ts in range(4):
                    dveY.w(sy, 4 * i + ts + 1)
                    d_y.inc(sy.dma_start(
                        out=xres[(i * 4 + ts) * 128:(i * 4 + ts + 1) * 128, :],
                        in_=yst[ts % 2][:]), 16)

        @block.tensor
        def _(te):
            join(te)
            d_wo.w(te, 256)
            te.wait_ge(g_const.h, 1)
            if bias:
                te.wait_ge(g_conv.h, CONV_BIAS)
            for i in range(8):
                p = i % 2
                d_z[p].w(te, 16 * (i // 2 + 1))
                for ts in range(4):
                    bset = (ts % 2) * 4
                    gts = 4 * i + ts
                    if gts >= 2:
                        dveY.w(te, gts - 1)
                    for j in range(4):
                        for h in range(16):
                            mi = te.matmul(
                                pb[bset + j][:],
                                zbl[p][:, h * 512 + ts * 128:
                                       h * 512 + (ts + 1) * 128],
                                wot[:, h * 2048 + j * 512:
                                    h * 2048 + (j + 1) * 512],
                                start=(h == 0),
                                stop=(h == 15 and not bias))
                        if bias:
                            mi = te.matmul(pb[bset + j][:],
                                           ones_row[0:1, 0:128],
                                           brow["o"][0:1,
                                                     j * 512:(j + 1) * 512],
                                           start=False, stop=True)
                        peY.inc(mi)

        @block.vector
        def _(v):
            join(v)
            for i in range(8):
                p = i % 2
                d_x[p].w(v, 16 * (i // 2 + 1))
                for ts in range(4):
                    bset = (ts % 2) * 4
                    gts = 4 * i + ts
                    if gts >= 2:
                        d_y.w(v, 16 * (gts - 1))
                    peY.w(v, 4 * gts + 4)
                    for j in range(4):
                        mi = v.tensor_tensor(
                            yst[ts % 2][:, j * 512:(j + 1) * 512],
                            pb[bset + j][:],
                            xrb[p][:, ts * 2048 + j * 512:
                                  ts * 2048 + (j + 1) * 512],
                            ALU.add)
                    dveY.inc(mi)

        es.close()
        return [d_y, dveY, peY, d_z[0], d_z[1], d_x[0], d_x[1]]

    def emit_ffn1(ffn_i):
        es = ExitStack()
        uidc[0] += 1
        _u = uidc[0]

        def sb(nm_, shape_, dt_):
            return nc.sbuf_tensor(f"{nm_}_{_u}", shape_, dt_)
        xnf = es.enter_context(sb("f1_xnf", [128, 16 * 2048], BF16))
        wgw = [es.enter_context(sb(f"f1_wg{i}", [128, 16 * 512], BF16)) for i in range(2)]
        wuw = [es.enter_context(sb(f"f1_wu{i}", [128, 16 * 512], BF16)) for i in range(2)]
        sg = es.enter_context(sb("f1_sg", [128, 4 * 512], F32))
        hh = es.enter_context(sb("f1_hh", [128, 4 * 512], F32))
        ha = es.enter_context(sb("f1_ha", [128, 4 * 512], F32))
        bcm = es.enter_context(sb("f1_bc", [128, 6 * 512], BF16))
        wn = es.enter_context(sb("f1_wn", [128, 4 * 512], BF16))
        hsp = [es.enter_context(sb(f"f1_hs{i}", [128, 4 * 512], BF16)) for i in range(2)]
        d_xn = mksem("f1_dxn")
        g_w = [mksem("f1_gw0"), mksem("f1_gw1")]
        d_h = [mksem("f1_dh0"), mksem("f1_dh1")]
        peG = mksem("f1_pe")
        acS = mksem("f1_ac")
        dveU = mksem("f1_dvu")
        dveH = mksem("f1_dvh")
        steps = [(hf, w, t4) for hf in range(2) for w in range(16)
                 for t4 in range(4)]

        @block.gpsimd
        def _(g):
            join(g)
            for hf in range(2):
                for w in range(16):
                    wi = hf * 16 + w
                    wp = wi % 2
                    if wi >= 2:
                        peG.w(g, 32 * (wi - 1))
                    for src, dstt in ((wg_in, wgw[wp]), (wu_in, wuw[wp])):
                        g_w[wp].inc(g.dma_start(
                            out=dstt[:].rearrange("p (dk o) -> p dk o", dk=16),
                            in_=src.rearrange("(dk p) o -> p dk o", p=128)
                            [:, :, w * 512:(w + 1) * 512]), 16)

        @block.sync
        def _(sy):
            join(sy)
            for si, (hf, w, t4) in enumerate(steps):
                if w == 0 and t4 == 0:
                    if hf > 0:
                        peG.w(sy, 8 * 64 * hf)
                    for dk in range(16):
                        d_xn.inc(sy.dma_start(
                            out=xnf[:, dk * 2048:(dk + 1) * 2048],
                            in_=xnfm[dk, :, hf * 2048:(hf + 1) * 2048]), 16)
                pt = si % 2
                dveH.w(sy, si + 1)
                d_h[pt].inc(sy.dma_start(
                    out=hscr[w * 4:w * 4 + 4, :,
                             hf * 2048 + t4 * 512:hf * 2048 + (t4 + 1) * 512]
                        .rearrange("c p t -> p c t"),
                    in_=hsp[pt][:].rearrange("p (c t) -> p c t", c=4)), 16)

        @block.tensor
        def _(te):
            join(te)
            for si, (hf, w, t4) in enumerate(steps):
                wi = hf * 16 + w
                wp = wi % 2
                d_xn.w(te, 256 * (hf + 1))
                g_w[wp].w(te, 32 * (wi // 2 + 1))
                if si > 0:
                    acS.w(te, 4 * si)
                for c in range(4):
                    for dk in range(16):
                        mi = te.matmul(
                            pb[c][:],
                            wgw[wp][:, dk * 512 + c: dk * 512 + 512: 4],
                            xnf[:, dk * 2048 + t4 * 512:
                                dk * 2048 + (t4 + 1) * 512],
                            start=(dk == 0), stop=(dk == 15))
                    peG.inc(mi)
                if si > 0:
                    dveU.w(te, 4 * si)
                for c in range(4):
                    for dk in range(16):
                        mi = te.matmul(
                            pb[4 + c][:],
                            wuw[wp][:, dk * 512 + c: dk * 512 + 512: 4],
                            xnf[:, dk * 2048 + t4 * 512:
                                dk * 2048 + (t4 + 1) * 512],
                            start=(dk == 0), stop=(dk == 15))
                    peG.inc(mi)

        @block.scalar
        def _(sc):
            join(sc)
            for si in range(len(steps)):
                for c in range(4):
                    peG.w(sc, 8 * si + c + 1)
                    if si > 0:
                        dveU.w(sc, 4 * (si - 1) + c + 1)
                    acS.inc(sc.activation(sg[:, c * 512:(c + 1) * 512],
                                          pb[c][:], AF.Silu))

        @block.vector
        def _(v):
            join(v)
            C = [slice(c * 512, (c + 1) * 512) for c in range(4)]
            prs = [(0, 1), (0, 2), (0, 3), (1, 2), (1, 3), (2, 3)]
            bi = {pr_: slice(k * 512, (k + 1) * 512)
                  for k, pr_ in enumerate(prs)}
            for si in range(len(steps)):
                pt = si % 2
                for c in range(4):
                    acS.w(v, 4 * si + c + 1)
                    peG.w(v, 8 * si + 4 + c + 1)
                    dveU.inc(v.tensor_tensor(hh[:, C[c]], pb[4 + c][:],
                                             sg[:, C[c]], ALU.mult))
                haI = ha[:].bitcast(mybir.dt.int32)
                hhI = hh[:].bitcast(mybir.dt.int32)
                CI = [slice(c * 512, (c + 1) * 512) for c in range(4)]
                for c in range(4):
                    v.tensor_scalar(haI[:, CI[c]], hhI[:, CI[c]], 0x7fffffff,
                                    None, ALU.bitwise_and)
                for (a_, b_) in prs:
                    v.tensor_tensor(bcm[:, bi[(a_, b_)]], ha[:, C[a_]],
                                    ha[:, C[b_]], ALU.is_ge)
                v.tensor_tensor(wn[:, C[0]], bcm[:, bi[(0, 1)]],
                                bcm[:, bi[(0, 2)]], ALU.add)
                v.tensor_tensor(wn[:, C[0]], wn[:, C[0]], bcm[:, bi[(0, 3)]],
                                ALU.add)
                v.scalar_tensor_tensor(wn[:, C[1]], bcm[:, bi[(0, 1)]], -1.0,
                                       bcm[:, bi[(1, 2)]], ALU.mult, ALU.add)
                v.tensor_tensor(wn[:, C[1]], wn[:, C[1]], bcm[:, bi[(1, 3)]],
                                ALU.add)
                v.scalar_tensor_tensor(wn[:, C[2]], bcm[:, bi[(0, 2)]], -1.0,
                                       bcm[:, bi[(1, 2)]], ALU.mult,
                                       ALU.subtract)
                v.tensor_tensor(wn[:, C[2]], wn[:, C[2]], bcm[:, bi[(2, 3)]],
                                ALU.add)
                v.scalar_tensor_tensor(wn[:, C[3]], bcm[:, bi[(0, 3)]],
                                       -1.0, bcm[:, bi[(1, 3)]], ALU.mult,
                                       ALU.subtract)
                v.tensor_tensor(wn[:, C[3]], wn[:, C[3]], bcm[:, bi[(2, 3)]],
                                ALU.subtract)
                if si >= 2:
                    d_h[pt].w(v, 16 * (si // 2))
                thr = [(2.0, ALU.is_ge), (1.0, ALU.is_ge), (0.0, ALU.is_ge),
                       (-1.0, ALU.is_ge)]
                for c in range(4):
                    t_, op_ = thr[c]
                    mi = v.scalar_tensor_tensor(hsp[pt][:, C[c]], wn[:, C[c]],
                                                t_, hh[:, C[c]], op_, ALU.mult)
                dveH.inc(mi)

        es.close()
        return [d_h[0], d_h[1], dveH, peG, acS, dveU, g_w[0], g_w[1], d_xn]

    def emit_ffn2(ffn_i, dst_ap, d_final):
        es = ExitStack()
        uidc[0] += 1
        _u = uidc[0]

        def sb(nm_, shape_, dt_):
            return nc.sbuf_tensor(f"{nm_}_{_u}", shape_, dt_)
        wdq = es.enter_context(sb("f2_wd", [128, 64 * 512], BF16))
        hbl = [es.enter_context(sb(f"f2_h{i}", [128, 32 * 512], BF16)) for i in range(2)]
        xq = [es.enter_context(sb(f"f2_x{i}", [128, 4 * 512], F32)) for i in range(2)]
        yst = [es.enter_context(sb(f"f2_y{i}", [128, 2 * 512], F32)) for i in range(2)]
        d_wd = mksem("f2_dwd")
        peH2 = mksem("f2_ph")
        d_hb = [mksem("f2_dh0"), mksem("f2_dh1")]
        d_x = [mksem("f2_dx0"), mksem("f2_dx1")]
        peY = mksem("f2_pe")
        dveY = mksem("f2_dv")
        fbase = d_final.n

        @block.sync
        def _(sy):
            join(sy)
            for q in range(4):
                for tb in range(8):
                    g = q * 8 + tb
                    if g >= 1:
                        peH2.w(sy, g)
                    d_hb[0].inc(sy.dma_start(
                        out=hbl[0][:]
                            .rearrange("p (fc t) -> p fc t", fc=32),
                        in_=hscr[0:32, :, tb * 512:(tb + 1) * 512]
                            .rearrange("fc p t -> p fc t")), 16)

        @block.scalar
        def _(sc):
            join(sc)
            sc.wait_ge(g_conv.h, CONV_WDP)
            for q in range(4):
                if q > 0:
                    peY.w(sc, 32 * q)
                for g4 in range(4):
                    d_wd.inc(sc.dma_start(
                        out=wdq[:, g4 * 16 * 512:(g4 + 1) * 16 * 512]
                            .rearrange("p (fc o) -> p fc o", fc=16),
                        in_=wdp[g4 * 16:(g4 + 1) * 16, :,
                                q * 512:(q + 1) * 512]
                            .rearrange("fc p o -> p fc o")), 16)
                for tb in range(8):
                    g = q * 8 + tb
                    if g >= 1:
                        peY.w(sc, 4 * g)
                    d_hb[1].inc(sc.dma_start(
                        out=hbl[1][:]
                            .rearrange("p (fc t) -> p fc t", fc=32),
                        in_=hscr[32:64, :, tb * 512:(tb + 1) * 512]
                            .rearrange("fc p t -> p fc t")), 16)

        @block.gpsimd
        def _(gp):
            join(gp)
            for q in range(4):
                for tb in range(8):
                    g = q * 8 + tb
                    pbuf = g % 2
                    if g >= 2:
                        dveY.w(gp, 4 * (g - 1))
                    d_x[pbuf].inc(gp.dma_start(
                        out=xq[pbuf][:].rearrange("p (ts o) -> p ts o", ts=4),
                        in_=xres[tb * 512:(tb + 1) * 512,
                                 q * 512:(q + 1) * 512]
                            .rearrange("(ts p) o -> p ts o", p=128)), 16)
                    for ts in range(4):
                        dveY.w(gp, 4 * g + ts + 1)
                        d_final.inc(gp.dma_start(
                            out=dst_ap[(tb * 4 + ts) * 128:
                                       (tb * 4 + ts + 1) * 128,
                                       q * 512:(q + 1) * 512],
                            in_=yst[ts % 2][:, (ts // 2) * 512:
                                            (ts // 2 + 1) * 512]), 16)

        @block.tensor
        def _(te):
            join(te)
            for q in range(4):
                te.wait_ge(d_wd.h, d_wd.base + 64 * (q + 1))
                for tb in range(8):
                    g = q * 8 + tb
                    for hhalf in range(2):
                        d_hb[hhalf].w(te, 16 * (g + 1))
                        for fc in range(32):
                            fcg = hhalf * 32 + fc
                            for ts in range(4):
                                bank = (tb % 2) * 4 + ts
                                gts = 4 * g + ts
                                if fc == 0 and hhalf == 0 and gts >= 8:
                                    dveY.w(te, gts - 7)
                                mi = te.matmul(
                                    pb[bank][:],
                                    hbl[hhalf][:, fc * 512 + ts * 128:
                                               fc * 512 + (ts + 1) * 128],
                                    wdq[:, fcg * 512:(fcg + 1) * 512],
                                    start=(fcg == 0), stop=(fcg == 63))
                                if fcg == 63:
                                    peY.inc(mi)
                            if fc == 31 and hhalf == 0:
                                peH2.inc(mi)

        @block.vector
        def _(v):
            join(v)
            for q in range(4):
                for tb in range(8):
                    g = q * 8 + tb
                    pbuf = g % 2
                    d_x[pbuf].w(v, 16 * (g // 2 + 1))
                    for ts in range(4):
                        bank = (tb % 2) * 4 + ts
                        gts = 4 * g + ts
                        peY.w(v, gts + 1)
                        if gts >= 4:
                            v.wait_ge(d_final.h, fbase + 16 * (gts - 3))
                        dveY.inc(v.tensor_tensor(
                            yst[ts % 2][:, (ts // 2) * 512:(ts // 2 + 1) * 512],
                            pb[bank][:],
                            xq[pbuf][:, ts * 512:(ts + 1) * 512],
                            ALU.add))

        es.close()
        return [d_final, dveY, peY, peH2, d_hb[0], d_hb[1], d_x[0], d_x[1],
                d_wd]

    d_out = mksem("d_out")
    d_mid = mksem("d_mid")

    order = [
        ("LN1", lambda: (lambda pg: (lambda fin: (pg[2].close(), fin)[1])(
            emit_ln(x_in, piggy=pg[0], mid_pool=lambda: emit_conv("a"),
                    post_pool=lambda: emit_conv("rest")) + pg[1]))(
            make_wtrans("a"))),
        ("QKV_a", lambda: emit_qkv("a")),
        ("ATT_a", lambda: emit_attn("a")),
        ("OP_a", lambda: emit_oproj("a", x_in)),
        ("LN2", lambda: emit_ln(xres)),
        ("F1P1", lambda: emit_ffn1(0)),
        ("F1P2", lambda: emit_ffn2(0, xres, d_mid)),
        ("LN3", lambda: (lambda pg: (lambda fin: (pg[2].close(), fin)[1])(
            emit_ln(xres, piggy=pg[0]) + pg[1]))(make_wtrans("b"))),
        ("QKV_b", lambda: emit_qkv("b")),
        ("ATT_b", lambda: emit_attn("b")),
        ("OP_b", lambda: emit_oproj("b", xres)),
        ("LN4", lambda: emit_ln(xres)),
        ("F2P1", lambda: emit_ffn1(1)),
        ("F2P2", lambda: emit_ffn2(1, y_out, d_out)),
    ]
    odict = dict(order)
    osel = [(n, odict[n]) for n in order_names] if order_names else order
    for name, fn in osel:
        finals = fn()
        if done(finals):
            break

    if tap is not None:
        tap_map = {
            "xn": (xnfm, [16, 128, S], BF16),
            "x1": (xres, [S, D], F32),
            "q": (qs, [NH, 128, S], BF16),
            "k": (ks, [NH, 128, S], BF16),
            "v": (vs, [NH, NT, 128, 128], BF16),
            "z": (zs, [NH, 128, S], BF16),
            "h": (hscr, [64, 128, S], BF16),
            "wt_qa": (wt["qa"], [D, D], BF16),
            "wdp": (wdp, [64, 128, D], BF16),
        }
        src, shp, dtt = tap_map[tap]
        tout = dt("tapout", shp, dtt, kind="ExternalOutput").ap()
        d_tap = mksem("d_tap")

        @block.sync
        def _(sy):
            join(sy)
            d_tap.inc(sy.dma_start(out=tout, in_=src), 16)
            d_tap.w(sy, 16)

    @block.sync
    def _(sy):
        join(sy)
        if d_out.n:
            d_out.w(sy, d_out.n)

    ges.close()
    return nc


def _shard_inputs(inputs):
    def r2(a):
        return np.ascontiguousarray(np.asarray(a, np.float32).reshape(D, D))

    def c1(a):
        return np.ascontiguousarray(np.asarray(a, np.float32))

    base = {
        "wqa": r2(inputs["L0_attn_a_W_Q"]),
        "wka": r2(inputs["L0_attn_a_W_K"]),
        "wva": r2(inputs["L0_attn_a_W_V"]),
        "woa": r2(inputs["L0_attn_a_W_O"]),
        "wqb": r2(inputs["L0_attn_b_Wq"]),
        "wkb": r2(inputs["L0_attn_b_Wk"]),
        "wvb": r2(inputs["L0_attn_b_Wv"]),
        "wob": r2(inputs["L0_attn_b_Wo"]),
        "bq": c1(inputs["L0_attn_b_bq"]),
        "bk": c1(inputs["L0_attn_b_bk"]),
        "bv": c1(inputs["L0_attn_b_bv"]),
        "bo": c1(inputs["L0_attn_b_bo"]),
        "wg": c1(inputs["L0_ffn_w_gate"]),
        "wu": c1(inputs["L0_ffn_w_up"]),
        "wd": c1(inputs["L0_ffn_w_down"]),
    }
    x = np.asarray(inputs["L0_attn_a_x"], np.float32)
    return [dict(base, x=np.ascontiguousarray(x[i])) for i in range(8)]


def kernel(**inputs):
    in_maps = _shard_inputs(inputs)
    nc = build_nc()
    res = run_bass_kernel_spmd(nc, in_maps, core_ids=list(range(8)))
    return np.stack([np.asarray(r["y"], np.float32) for r in res.results])


# version 13
# speedup vs baseline: 1.0157x; 1.0157x over previous
import math
from contextlib import ExitStack

import numpy as np
import ml_dtypes

import concourse.bass as bass
import concourse.mybir as mybir
from concourse.bass_utils import run_bass_kernel_spmd

F32 = mybir.dt.float32
BF16 = mybir.dt.bfloat16
AF = mybir.ActivationFunctionType
ALU = mybir.AluOpType
AX = mybir.AxisListType
BFNP = ml_dtypes.bfloat16

S = 4096
D = 2048
QKV_STORES = "all"
CONV_PARTS = "all"
LN_DBG = [None]
NH = 16
FI = 8192
NT = S // 128
SC_ATT = 1.0 / math.sqrt(128.0)
ESHIFT = -12.0
EPS = 1e-5


class Sem:
    def __init__(self, nc, es, name):
        self.h = es.enter_context(nc.semaphore(name))
        self.n = 0
        self.base = 0

    def inc(self, instr, k=1):
        instr.then_inc(self.h, k)
        self.n += k
        return instr

    def w(self, eng, v):
        eng.wait_ge(self.h, self.base + v)


def build_nc(upto=99, tap=None, dbg=False, order_names=None):
    nc = bass.Bass("TRN2", target_bir_lowering=False)
    dt = nc.dram_tensor

    x_in = dt("x", [S, D], F32, kind="ExternalInput").ap()
    w_att = {}
    for fl in ("a", "b"):
        for w in ("q", "k", "v", "o"):
            w_att[w + fl] = dt(f"w{w}{fl}", [D, D], F32, kind="ExternalInput").ap()
    b_att = {w: dt(f"b{w}", [D], F32, kind="ExternalInput").ap()
             for w in ("q", "k", "v", "o")}
    wg_in = dt("wg", [D, FI], F32, kind="ExternalInput").ap()
    wu_in = dt("wu", [D, FI], F32, kind="ExternalInput").ap()
    wd_in = dt("wd", [FI, D], F32, kind="ExternalInput").ap()
    y_out = dt("y", [S, D], F32, kind="ExternalOutput").ap()

    xres = dt("xres", [S, D], F32).ap()
    xbf = dt("xbf", [S, D], BF16).ap()
    xnfm = dt("xnfm", [16, 128, S], BF16).ap()
    wc = {k: dt(f"wc_{k}", [D, D], BF16).ap() for k in w_att}
    wt = {k: dt(f"wt_{k}", [D, D], BF16).ap()
          for k in ("qa", "ka", "va", "qb", "kb", "vb")}
    qs = dt("qs", [NH, 128, S], BF16).ap()
    ks = dt("ks", [NH, 128, S], BF16).ap()
    vs = dt("vs", [NH, NT, 128, 128], BF16).ap()
    zs = dt("zs", [NH, 128, S], BF16).ap()
    wdp = dt("wdp", [64, 128, D], BF16).ap()
    hscr = dt("hscr", [64, 128, S], BF16).ap()

    ges = ExitStack()
    block = ges.enter_context(nc.Block())

    semcache = {}

    def mksem(name):
        s = semcache.get(name)
        if s is None:
            s = Sem(nc, ges, name)
            semcache[name] = s
        s.base = s.n
        return s

    prev = [[]]
    uidc = [0]

    def join(eng):
        for s, v in prev[0]:
            if v > 0:
                eng.wait_ge(s.h, v)

    ones_col = ges.enter_context(nc.sbuf_tensor("ones_col", [128, 8], BF16))
    ones_row = ges.enter_context(nc.sbuf_tensor("ones_row", [1, 512], BF16))
    ones_row_f = ges.enter_context(nc.sbuf_tensor("ones_row_f", [1, 128], F32))
    maskbig = ges.enter_context(nc.sbuf_tensor("maskbig", [128, 896], BF16))
    iota_s = ges.enter_context(nc.sbuf_tensor("iota_s", [128, 896], F32))
    brow = {w: ges.enter_context(nc.sbuf_tensor(f"brow_{w}", [1, D], BF16))
            for w in ("q", "k", "v", "o")}
    pb = [ges.enter_context(nc.psum_tensor(f"pb{i}", [128, 512], F32))
          for i in range(8)]
    cconst = ges.enter_context(nc.sbuf_tensor("cconst", [128, 4], F32))

    g_const = mksem("g_const")
    g_conv = mksem("g_conv")

    @block.gpsimd
    def _(g):
        g.memset(ones_col[:], 1.0)
        g.memset(cconst[:, 0:1], EPS)
        g.memset(cconst[:, 1:2], ESHIFT)
        g.memset(ones_row[:], 1.0)
        g.memset(ones_row_f[:], 1.0)
        g_const.inc(g.iota(iota_s[:], pattern=[[1, 896]], base=0,
                           channel_multiplier=-1,
                           allow_small_or_imprecise_dtypes=True), 1)

    def emit_conv(part="all"):
        @block.gpsimd
        def _(g):
            if CONV_PARTS == "none":
                return
            if part in ("a", "all"):
                for k in ("qa", "ka", "va", "oa"):
                    g_conv.inc(g.dma_start(out=wc[k], in_=w_att[k]), 16)
            if part == "a":
                return
            for k in ("qb", "kb", "vb", "ob"):
                g_conv.inc(g.dma_start(out=wc[k], in_=w_att[k]), 16)
            if CONV_PARTS == "wc":
                return
            for w in ("q", "k", "v", "o"):
                g_conv.inc(g.dma_start(
                    out=brow[w][:],
                    in_=b_att[w].rearrange("(a n) -> a n", a=1)), 16)
            if CONV_PARTS == "wcbias":
                return
            for fc in range(64):
                w_, c_ = fc // 4, fc % 4
                g_conv.inc(g.dma_start(
                    out=wdp[fc],
                    in_=wd_in[w_ * 512 + c_: w_ * 512 + c_ + 509: 4, :]), 16)

    @block.vector
    def _(v):
        v.wait_ge(g_const.h, 1)
        g_const.inc(v.tensor_scalar(maskbig[:], iota_s[:], 384.0, None,
                                    ALU.is_ge), 1)

    CONV_A = 4 * 16
    CONV_B = 8 * 16
    CONV_BIAS = 12 * 16
    CONV_WDP = CONV_BIAS + 64 * 16
    nphase = [0]

    def done(finals):
        prev[0] = [(s, s.n) for s in finals]
        nphase[0] += 1
        return nphase[0] >= upto

    def emit_ln(src_ap, piggy=None, mid_pool=None, post_pool=None):
        es = ExitStack()
        uidc[0] += 1
        _u = uidc[0]

        def sb(nm_, shape_, dt_):
            return nc.sbuf_tensor(f"{nm_}_{_u}", shape_, dt_)
        xfm = es.enter_context(sb("ln_xf", [128, 16 * 2048], BF16))
        sqf = es.enter_context(sb("ln_sq", [128, 16 * 1024], BF16))
        rows = es.enter_context(sb("ln_rows", [1, 3 * 1024], F32))
        rowb = es.enter_context(sb("ln_rowb", [1, 2 * 1024], BF16))
        bcst = es.enter_context(sb("ln_bc", [128, 2 * 1024], F32))
        xnd = [es.enter_context(sb(f"ln_xd{i}", [128, 1024], BF16)) for i in range(2)]
        xng = [es.enter_context(sb(f"ln_xg{i}", [128, 1024], BF16)) for i in range(2)]
        t1f = es.enter_context(sb("ln_t1", [128, 1024], F32))
        t1g = es.enter_context(sb("ln_t2", [128, 1024], F32))
        d_c = mksem("ln_dc")
        d_x = mksem("ln_dx")
        d_o = mksem("ln_do")
        a_q = mksem("ln_aq")
        a_r = mksem("ln_arow")
        a_e = mksem("ln_ae")
        peS = mksem("ln_pes")
        peB = mksem("ln_peb")
        d_m = mksem("ln_dm")
        d_n = mksem("ln_dn")
        d_g = mksem("ln_dg")
        NB = 4

        def xsl(b, dk):
            return xfm[:, dk * 2048 + (b % 2) * 1024:
                       dk * 2048 + (b % 2) * 1024 + 1024]

        def emit_evac(sc, b):
            peB.w(sc, b + 1)
            if b >= 1:
                d_n.w(sc, 8 * b)
                d_g.w(sc, 8 * b)
            sc.activation(bcst[:, 0:512], pb[4][:], AF.Copy)
            sc.activation(bcst[:, 512:1024], pb[5][:], AF.Copy)
            sc.activation(bcst[:, 1024:1536], pb[6][:], AF.Copy)
            mi = sc.activation(bcst[:, 1536:2048], pb[7][:], AF.Copy)
            a_e.inc(mi)

        @block.gpsimd
        def _(g):
            join(g)
            for c in range(NB):
                d_c.inc(g.dma_start(out=xbf[c * 1024:(c + 1) * 1024, :],
                                    in_=src_ap[c * 1024:(c + 1) * 1024, :]),
                        16)

        if mid_pool is not None:
            mid_pool()

        @block.gpsimd
        def _(g):
            for b in range(NB):
                a_e.w(g, b + 1)
                for dk in range(8, 16):
                    if b * 16 + dk >= 2:
                        d_o.w(g, 16 * (16 * b + dk - 1))
                    g.tensor_tensor(t1g[:], xsl(b, dk), bcst[:, 0:1024],
                                    ALU.add)
                    d_g.inc(g.tensor_tensor(xng[dk % 2][:], t1g[:],
                                            bcst[:, 1024:2048], ALU.mult))

        if post_pool is not None:
            post_pool()

        def emit_stores(sy, b):
            for dk in range(16):
                if dk < 8:
                    d_n.w(sy, 8 * b + dk + 1)
                    srcb = xnd[dk % 2]
                else:
                    d_g.w(sy, 8 * b + (dk - 8) + 1)
                    srcb = xng[dk % 2]
                d_o.inc(sy.dma_start(
                    out=xnfm[dk, :, b * 1024:(b + 1) * 1024],
                    in_=srcb[:]), 16)

        @block.sync
        def _(sy):
            join(sy)
            for sbk in range(2):
                d_c.w(sy, 32 * (sbk + 1))
                if sbk > 0:
                    d_n.w(sy, 16)
                    d_g.w(sy, 16)
                for dk in range(16):
                    d_x.inc(sy.dma_start_transpose(
                        out=xfm[:, dk * 2048:(dk + 1) * 2048],
                        in_=xbf[sbk * 2048:(sbk + 1) * 2048,
                                dk * 128:(dk + 1) * 128]), 16)
                for bb in range(2):
                    emit_stores(sy, sbk * 2 + bb)

        @block.scalar
        def _(sc):
            join(sc)
            sc.wait_ge(g_const.h, 1)
            for b in range(NB):
                if b >= 1:
                    emit_evac(sc, b - 1)
                d_x.w(sc, 256 * (b // 2 + 1))
                if b >= 1:
                    peS.w(sc, b)
                for dk in range(16):
                    mi = sc.activation(sqf[:, dk * 1024:(dk + 1) * 1024],
                                       xsl(b, dk), AF.Square)
                a_q.inc(mi)
                d_m.w(sc, 2 * b + 1)
                sc.activation(rows[0:1, 1024:2048], rows[0:1, 2048:3072],
                              AF.Ln, bias=cconst[0:1, 0:1])
                if b >= 1:
                    peB.w(sc, b)
                mi = sc.activation(rowb[0:1, 1024:2048], rows[0:1, 1024:2048],
                                   AF.Exp, scale=-0.5)
                a_r.inc(mi)
                if piggy is not None:
                    piggy(sc, b)
            emit_evac(sc, NB - 1)

        @block.tensor
        def _(te):
            join(te)
            te.wait_ge(g_const.h, 1)
            for b in range(NB):
                d_x.w(te, 256 * (b // 2 + 1))
                if b >= 1:
                    d_m.w(te, 2 * b)
                for half in range(2):
                    for dk in range(16):
                        mi = te.matmul(pb[0 + half][0:1, :], ones_col[:, 0:1],
                                       xsl(b, dk)[:, half * 512:
                                                  half * 512 + 512],
                                       start=(dk == 0), stop=(dk == 15))
                a_q.w(te, b + 1)
                for half in range(2):
                    for dk in range(16):
                        mi = te.matmul(pb[2 + half][0:1, :], ones_col[:, 0:1],
                                       sqf[:, dk * 1024 + half * 512:
                                           dk * 1024 + half * 512 + 512],
                                       start=(dk == 0), stop=(dk == 15))
                peS.inc(mi)
                d_m.w(te, 2 * b + 2)
                if b >= 1:
                    a_e.w(te, b)
                te.matmul(pb[4][:], ones_row[0:1, 0:128],
                          rowb[0:1, 0:512], start=True, stop=True)
                te.matmul(pb[5][:], ones_row[0:1, 0:128],
                          rowb[0:1, 512:1024], start=True, stop=True)
                a_r.w(te, b + 1)
                te.matmul(pb[6][:], ones_row[0:1, 0:128],
                          rowb[0:1, 1024:1536], start=True, stop=True)
                mi = te.matmul(pb[7][:], ones_row[0:1, 0:128],
                               rowb[0:1, 1536:2048], start=True, stop=True)
                peB.inc(mi)

        @block.vector
        def _(v):
            join(v)
            for b in range(NB):
                peS.w(v, b + 1)
                v.tensor_scalar(rows[0:1, 0:512], pb[0][0:1, :], -1.0 / D,
                                None, ALU.mult)
                v.tensor_scalar(rows[0:1, 512:1024], pb[1][0:1, :], -1.0 / D,
                                None, ALU.mult)
                v.tensor_tensor(rows[0:1, 1024:2048], rows[0:1, 0:1024],
                                rows[0:1, 0:1024], ALU.mult)
                v.tensor_scalar(rows[0:1, 2048:2560], pb[2][0:1, :], 1.0 / D,
                                None, ALU.mult)
                v.tensor_scalar(rows[0:1, 2560:3072], pb[3][0:1, :], 1.0 / D,
                                None, ALU.mult)
                d_m.inc(v.tensor_tensor(rows[0:1, 2048:3072],
                                        rows[0:1, 2048:3072],
                                        rows[0:1, 1024:2048], ALU.subtract))
                if b >= 1:
                    peB.w(v, b)
                d_m.inc(v.tensor_copy(rowb[0:1, 0:1024], rows[0:1, 0:1024]))
                a_e.w(v, b + 1)
                for dk in range(8):
                    if b * 16 + dk >= 2:
                        d_o.w(v, 16 * (16 * b + dk - 1))
                    v.tensor_tensor(t1f[:], xsl(b, dk), bcst[:, 0:1024],
                                    ALU.add)
                    d_n.inc(v.tensor_tensor(xnd[dk % 2][:], t1f[:],
                                            bcst[:, 1024:2048], ALU.mult))

        es.close()
        return [d_o, d_n, d_g, a_e, peB, d_c]

    def make_wtrans(fl):
        es = ExitStack()
        uidc[0] += 1
        _u = uidc[0]
        tb = [es.enter_context(nc.sbuf_tensor(f"wt_tb{i}_{_u}", [128, D], BF16))
              for i in range(12)]
        d_t = mksem("wt_dt")
        d_s = mksem("wt_ds")
        jobs = [(w, dk) for w in ("q", "k", "v") for dk in range(16)]

        def piggy(sy, slot):
            if slot == 0:
                sy.wait_ge(g_conv.h, CONV_A if fl == "a" else CONV_B)
            for j in range(slot * 12, slot * 12 + 12):
                w, dk = jobs[j]
                if j >= 12:
                    d_s.w(sy, 16 * (j - 11))
                d_t.inc(sy.dma_start_transpose(
                    out=tb[j % 12][:],
                    in_=wc[w + fl][:, dk * 128:(dk + 1) * 128]), 16)
            for j in range(slot * 12, slot * 12 + 12):
                w, dk = jobs[j]
                d_t.w(sy, 16 * (j + 1))
                d_s.inc(sy.dma_start(
                    out=wt[w + fl][dk * 128:(dk + 1) * 128, :],
                    in_=tb[j % 12][:]), 16)

        return piggy, [d_s], es

    def emit_qkv(fl):
        es = ExitStack()
        uidc[0] += 1
        _u = uidc[0]

        def sb(nm_, shape_, dt_):
            return nc.sbuf_tensor(f"{nm_}_{_u}", shape_, dt_)
        xnf = es.enter_context(sb("qk_xnf", [128, 16 * 2048], BF16))
        ws = {w: [es.enter_context(sb(f"qk_w{w}{i}", [128, 16 * 512], BF16))
                  for i in range(2)] for w in ("q", "k", "v")}
        qst = [es.enter_context(sb(f"qk_q{i}", [128, 4 * 512], BF16)) for i in range(2)]
        kst = [es.enter_context(sb(f"qk_k{i}", [128, 4 * 512], BF16)) for i in range(2)]
        vst = [es.enter_context(sb(f"qk_v{i}", [128, 4 * 512], BF16)) for i in range(2)]
        d_xn = mksem("qk_dxn")
        d_w = mksem("qk_dw")
        d_st = [mksem("qk_ds0"), mksem("qk_ds1")]
        peQ = mksem("qk_pe")
        acQ = mksem("qk_ac")
        bias = (fl == "b")
        steps = [(hf, hb, t4) for hf in range(2) for hb in range(4)
                 for t4 in range(4)]

        @block.sync
        def _(sy):
            join(sy)
            for si, (hf, hb, t4) in enumerate(steps):
                p = si % 2
                if hb == 0 and t4 == 0:
                    if hf > 0:
                        peQ.w(sy, 12 * si)
                    for dk in range(16):
                        d_xn.inc(sy.dma_start(
                            out=xnf[:, dk * 2048:(dk + 1) * 2048],
                            in_=xnfm[dk, :, hf * 2048:(hf + 1) * 2048]), 16)
                if t4 == 0:
                    gw = si // 4
                    if gw >= 2:
                        peQ.w(sy, 48 * (gw - 1))
                    for w in ("q", "k", "v"):
                        d_w.inc(sy.dma_start(
                            out=ws[w][gw % 2][:]
                                .rearrange("p (dk o) -> p dk o", dk=16),
                            in_=wt[w + fl].rearrange("(dk p) o -> p dk o",
                                                     p=128)
                            [:, :, hb * 512:(hb + 1) * 512]), 16)
                h0 = hb * 4
                tg = hf * 4 + t4
                acQ.w(sy, 3 * si + 1)
                if QKV_STORES in ("all", "qk"):
                    d_st[p].inc(sy.dma_start(
                        out=qs[h0:h0 + 4, :, tg * 512:(tg + 1) * 512]
                            .rearrange("h p t -> p h t"),
                        in_=qst[p][:].rearrange("p (h t) -> p h t", h=4)), 16)
                acQ.w(sy, 3 * si + 2)
                if QKV_STORES in ("all", "qk"):
                    d_st[p].inc(sy.dma_start(
                        out=ks[h0:h0 + 4, :, tg * 512:(tg + 1) * 512]
                            .rearrange("h p t -> p h t"),
                        in_=kst[p][:].rearrange("p (h t) -> p h t", h=4)), 16)
                acQ.w(sy, 3 * si + 3)
                if QKV_STORES in ("all", "v"):
                    for ts in range(4):
                        d_st[p].inc(sy.dma_start(
                            out=vs[h0:h0 + 4, tg * 4 + ts, :, :]
                                .rearrange("h t dh -> t h dh"),
                            in_=vst[p][:, ts * 512:(ts + 1) * 512]
                                .rearrange("t (h dh) -> t h dh", h=4)), 16)
            if dbg:
                peQ.w(sy, 12 * 32)
                d_st[0].inc(sy.dma_start(out=dbg_xnf, in_=xnf[:]), 16)
                d_st[0].inc(sy.dma_start(out=dbg_wq, in_=ws["q"][:]), 16)
                d_st[0].inc(sy.dma_start(out=dbg_qst, in_=qst[1][:]), 16)

        @block.tensor
        def _(te):
            join(te)
            te.wait_ge(g_const.h, 1)
            if bias:
                te.wait_ge(g_conv.h, CONV_BIAS)
            for si, (hf, hb, t4) in enumerate(steps):
                h0 = hb * 4
                gwp = (si // 4) % 2
                d_xn.w(te, 256 * (hf + 1))
                d_w.w(te, 48 * (si // 4 + 1))
                if si > 0:
                    acQ.w(te, 3 * si)
                for h in range(4):
                    for dk in range(16):
                        mi = te.matmul(pb[h][:],
                                       ws["q"][gwp][:, dk * 512 + h * 128:
                                               dk * 512 + (h + 1) * 128],
                                       xnf[:, dk * 2048 + t4 * 512:
                                           dk * 2048 + (t4 + 1) * 512],
                                       start=(dk == 0),
                                       stop=(dk == 15 and not bias))
                    if bias:
                        mi = te.matmul(pb[h][:],
                                       brow["q"][0:1, (h0 + h) * 128:
                                                 (h0 + h + 1) * 128],
                                       ones_row[0:1, :],
                                       start=False, stop=True)
                    peQ.inc(mi)
                for h in range(4):
                    for dk in range(16):
                        mi = te.matmul(pb[4 + h][:],
                                       ws["k"][gwp][:, dk * 512 + h * 128:
                                               dk * 512 + (h + 1) * 128],
                                       xnf[:, dk * 2048 + t4 * 512:
                                           dk * 2048 + (t4 + 1) * 512],
                                       start=(dk == 0),
                                       stop=(dk == 15 and not bias))
                    if bias:
                        mi = te.matmul(pb[4 + h][:],
                                       brow["k"][0:1, (h0 + h) * 128:
                                                 (h0 + h + 1) * 128],
                                       ones_row[0:1, :],
                                       start=False, stop=True)
                    peQ.inc(mi)
                acQ.w(te, 3 * si + 1)
                for ts in range(4):
                    for dk in range(16):
                        mi = te.matmul(pb[ts][:],
                                       xnf[:, dk * 2048 + t4 * 512 + ts * 128:
                                           dk * 2048 + t4 * 512 + (ts + 1) * 128],
                                       ws["v"][gwp][:, dk * 512:(dk + 1) * 512],
                                       start=(dk == 0),
                                       stop=(dk == 15 and not bias))
                    if bias:
                        mi = te.matmul(pb[ts][:],
                                       ones_row[0:1, 0:128],
                                       brow["v"][0:1, h0 * 128:h0 * 128 + 512],
                                       start=False, stop=True)
                    peQ.inc(mi)

        @block.scalar
        def _(sc):
            join(sc)
            for si in range(32):
                p = si % 2
                if si >= 2:
                    nst = {"all": 6, "qk": 2, "v": 4, "none": 0}[QKV_STORES]
                    if nst:
                        d_st[p].w(sc, 16 * nst * (si // 2))
                peQ.w(sc, 12 * si + 4)
                for h in range(4):
                    mi = sc.activation(qst[p][:, h * 512:(h + 1) * 512],
                                       pb[h][:], AF.Copy)
                acQ.inc(mi)
                peQ.w(sc, 12 * si + 8)
                for h in range(4):
                    mi = sc.activation(kst[p][:, h * 512:(h + 1) * 512],
                                       pb[4 + h][:], AF.Copy)
                acQ.inc(mi)
                peQ.w(sc, 12 * si + 12)
                for ts in range(4):
                    mi = sc.activation(vst[p][:, ts * 512:(ts + 1) * 512],
                                       pb[ts][:], AF.Copy)
                acQ.inc(mi)

        es.close()
        return [d_st[0], d_st[1], peQ, acQ, d_xn, d_w]

    if dbg:
        dbg_xnf = dt("dbg_xnf", [128, 16 * 2048], BF16, kind="ExternalOutput").ap()
        dbg_wq = dt("dbg_wq", [128, 16 * 512], BF16, kind="ExternalOutput").ap()
        dbg_qst = dt("dbg_qst", [128, 4 * 512], BF16, kind="ExternalOutput").ap()

    def emit_attn(fl):
        es = ExitStack()
        uidc[0] += 1
        _u = uidc[0]

        def sb(nm_, shape_, dt_):
            return nc.sbuf_tensor(f"{nm_}_{_u}", shape_, dt_)
        Kb = [es.enter_context(sb(f"at_k{i}", [128, S], BF16)) for i in range(2)]
        Qb = [es.enter_context(sb(f"at_q{i}", [128, S], BF16)) for i in range(2)]
        Vb = [es.enter_context(sb(f"at_v{i}", [128, S], BF16)) for i in range(2)]
        PT = [es.enter_context(sb(f"at_pt{i}", [128, 512], BF16)) for i in range(6)]
        linv = [es.enter_context(sb(f"at_li{i}", [1, 512], F32)) for i in range(2)]
        lsc = es.enter_context(sb("at_lsc", [1, 512], F32))
        invb = [es.enter_context(sb(f"at_iv{i}", [128, 512], F32)) for i in range(2)]
        zst = [es.enter_context(sb(f"at_z{i}", [128, 512], BF16)) for i in range(2)]
        d_kv = [mksem("at_dk0"), mksem("at_dk1")]
        d_z = [mksem("at_dz0"), mksem("at_dz1")]
        peS = mksem("at_peS")
        peA = mksem("at_peA")
        peL = mksem("at_peL")
        peB = mksem("at_peB")
        peH = mksem("at_peH")
        acE = mksem("at_acE")
        acB = mksem("at_acB")
        dveM = mksem("at_dvM")
        dveR = mksem("at_dvR")
        dveZ = mksem("at_dvZ")

        def kbase(h, qb):
            return h * 144 + 2 * qb * qb + 2 * qb

        def emit_loads(sy, h):
            p = h % 2
            d_kv[p].inc(sy.dma_start(out=Kb[p][:], in_=ks[h]), 16)
            d_kv[p].inc(sy.dma_start(out=Qb[p][:], in_=qs[h]), 16)
            for piece in range(4):
                d_kv[p].inc(sy.dma_start(
                    out=Vb[p][:, piece * 1024:(piece + 1) * 1024]
                        .rearrange("t (tb dh) -> t tb dh", tb=8),
                    in_=vs[h, piece * 8:(piece + 1) * 8, :, :]
                        .rearrange("tb t dh -> t tb dh")), 16)

        @block.sync
        def _(sy):
            join(sy)
            emit_loads(sy, 0)
            emit_loads(sy, 1)
            for h in range(NH):
                for qb in range(8):
                    gz = h * 8 + qb
                    pq = gz % 2
                    dveZ.w(sy, gz + 1)
                    d_z[pq].inc(sy.dma_start(
                        out=zs[h][:, qb * 512:(qb + 1) * 512],
                        in_=zst[pq][:]), 16)
                if h + 2 < NH:
                    peH.w(sy, h + 1)
                    emit_loads(sy, h + 2)

        @block.tensor
        def _(te):
            join(te)
            te.wait_ge(g_const.h, 2)

            def pe_bcast(i):
                dveR.w(te, i + 1)
                if i >= 1:
                    acB.w(te, i)
                peB.inc(te.matmul(pb[6][:], ones_row_f[0:1, :],
                                  linv[i % 2][0:1, :], start=True, stop=True))

            for h in range(NH):
                p = h % 2
                d_kv[p].w(te, 96 * (h // 2 + 1))
                for qb in range(8):
                    gz = h * 8 + qb
                    kn = 4 * qb + 4
                    eb = kbase(h, qb)
                    if gz >= 2:
                        dveZ.w(te, gz - 1)
                        dveR.w(te, gz - 1)
                    bursts = []
                    s0 = 0
                    while s0 < kn:
                        bursts.append((s0, min(3, kn - s0)))
                        s0 += 3

                    def consume(c0, c1, kn=kn, qb=qb, gz=gz, eb=eb, p=p):
                        ne_hi = min(c1, 4 * qb - 1)
                        if ne_hi >= c0:
                            acE.w(te, eb + ne_hi + 1)
                        for c in range(c0, c1 + 1):
                            if c >= 4 * qb:
                                dveM.w(te, 4 * gz + (c - 4 * qb) + 1)
                            mi = te.matmul(pb[4 + gz % 2][0:1, :],
                                           ones_col[:, 0:1], PT[c % 6][:],
                                           start=(c == 0), stop=(c == kn - 1))
                            if c == kn - 1:
                                peL.inc(mi)
                            mi = te.matmul(pb[2 + gz % 2][:],
                                           Vb[p][:, c * 128:(c + 1) * 128],
                                           PT[c % 6][:],
                                           start=(c == 0),
                                           stop=(c == kn - 1))
                            if c % 2 == 1 or c == kn - 1:
                                peA.inc(mi)

                    for bi, (s0, sz) in enumerate(bursts):
                        for kb in range(s0, s0 + sz):
                            if kb >= 3:
                                acE.w(te, eb + kb - 2)
                            peS.inc(te.matmul(
                                pb[(0, 1, 7)[kb % 3]][:],
                                Kb[p][:, kb * 128:(kb + 1) * 128],
                                Qb[p][:, qb * 512:(qb + 1) * 512],
                                start=True, stop=True))
                        if gz >= 1 and (s0 == 6 or (kn == 4 and s0 == 3)):
                            pe_bcast(gz - 1)
                        if bi > 0:
                            ps0, psz = bursts[bi - 1]
                            consume(ps0, ps0 + psz - 1)
                    ls0, lsz = bursts[-1]
                    consume(ls0, ls0 + lsz - 1)
                peH.inc(te.nop())
            pe_bcast(NH * 8 - 1)

        @block.scalar
        def _(sc):
            join(sc)
            sc.wait_ge(g_const.h, 1)

            def ac_norm(i):
                peB.w(sc, i + 1)
                if i >= 2:
                    dveZ.w(sc, i - 1)
                acB.inc(sc.activation(invb[i % 2][:], pb[6][:], AF.Copy))

            for h in range(NH):
                for qb in range(8):
                    gz = h * 8 + qb
                    kn = 4 * qb + 4
                    eb = kbase(h, qb)
                    for c in range(kn):
                        peS.w(sc, eb + c + 1)
                        if c >= 6:
                            peA.w(sc, eb // 2 + (c - 6) // 2 + 1)
                        acE.inc(sc.activation(PT[c % 6][:],
                                              pb[(0, 1, 7)[c % 3]][:],
                                              AF.Exp, bias=cconst[:, 1:2],
                                              scale=SC_ATT))
                        if c == min(kn - 1, 7) - 2 and gz >= 1:
                            ac_norm(gz - 1)
            ac_norm(NH * 8 - 1)

        @block.vector
        def _(v):
            join(v)

            def dv_z(i):
                acB.w(v, i + 1)
                pq = i % 2
                if i >= 2:
                    d_z[pq].w(v, 16 * (i // 2))
                dveZ.inc(v.tensor_tensor(zst[pq][:], pb[2 + i % 2][:],
                                         invb[i % 2][:], ALU.mult))

            for h in range(NH):
                for qb in range(8):
                    gz = h * 8 + qb
                    kn = 4 * qb + 4
                    eb = kbase(h, qb)
                    for r in range(4):
                        c = 4 * qb + r
                        acE.w(v, eb + c + 1)
                        dveM.inc(v.tensor_tensor(
                            PT[c % 6][:], PT[c % 6][:],
                            maskbig[:, 384 - 128 * r: 896 - 128 * r],
                            ALU.mult))
                    if gz >= 1:
                        dv_z(gz - 1)
                    peL.w(v, gz + 1)
                    if gz >= 2:
                        peB.w(v, gz - 1)
                    dveR.inc(v.reciprocal(linv[gz % 2][:],
                                          pb[4 + gz % 2][0:1, :]))
            dv_z(NH * 8 - 1)

        es.close()
        return [d_z[0], d_z[1], peH, dveZ, acB, peB, d_kv[0], d_kv[1]]

    def emit_oproj(fl, res_src):
        es = ExitStack()
        uidc[0] += 1
        _u = uidc[0]

        def sb(nm_, shape_, dt_):
            return nc.sbuf_tensor(f"{nm_}_{_u}", shape_, dt_)
        wot = es.enter_context(sb("op_wot", [128, 16 * 2048], BF16))
        zbl = [es.enter_context(sb(f"op_z{i}", [128, 16 * 512], BF16)) for i in range(2)]
        xrb = [es.enter_context(sb(f"op_x{i}", [128, 4 * 2048], F32)) for i in range(2)]
        yst = [es.enter_context(sb(f"op_y{i}", [128, 2048], F32)) for i in range(2)]
        d_wo = mksem("op_dwo")
        d_z = [mksem("op_dz0"), mksem("op_dz1")]
        d_x = [mksem("op_dx0"), mksem("op_dx1")]
        d_y = mksem("op_dy")
        peY = mksem("op_pe")
        dveY = mksem("op_dv")
        bias = (fl == "b")

        @block.sync
        def _(sy):
            join(sy)
            sy.wait_ge(g_conv.h, CONV_A if fl == "a" else CONV_B)
            for dk in range(16):
                d_wo.inc(sy.dma_start_transpose(
                    out=wot[:, dk * 2048:(dk + 1) * 2048],
                    in_=wc["o" + fl][:, dk * 128:(dk + 1) * 128]), 16)
            for i in range(8):
                p = i % 2
                if i >= 2:
                    dveY.w(sy, 4 * (i - 1))
                d_z[p].inc(sy.dma_start(
                    out=zbl[p][:].rearrange("p (h t) -> p h t", h=16),
                    in_=zs[:, :, i * 512:(i + 1) * 512]
                        .rearrange("h p t -> p h t")), 16)
                d_x[p].inc(sy.dma_start(
                    out=xrb[p][:].rearrange("p (ts d2) -> p ts d2", ts=4),
                    in_=res_src[i * 512:(i + 1) * 512, :]
                        .rearrange("(ts p) d2 -> p ts d2", p=128)), 16)
                for ts in range(4):
                    dveY.w(sy, 4 * i + ts + 1)
                    d_y.inc(sy.dma_start(
                        out=xres[(i * 4 + ts) * 128:(i * 4 + ts + 1) * 128, :],
                        in_=yst[ts % 2][:]), 16)

        @block.tensor
        def _(te):
            join(te)
            d_wo.w(te, 256)
            te.wait_ge(g_const.h, 1)
            if bias:
                te.wait_ge(g_conv.h, CONV_BIAS)
            for i in range(8):
                p = i % 2
                d_z[p].w(te, 16 * (i // 2 + 1))
                for ts in range(4):
                    bset = (ts % 2) * 4
                    gts = 4 * i + ts
                    if gts >= 2:
                        dveY.w(te, gts - 1)
                    for j in range(4):
                        for h in range(16):
                            mi = te.matmul(
                                pb[bset + j][:],
                                zbl[p][:, h * 512 + ts * 128:
                                       h * 512 + (ts + 1) * 128],
                                wot[:, h * 2048 + j * 512:
                                    h * 2048 + (j + 1) * 512],
                                start=(h == 0),
                                stop=(h == 15 and not bias))
                        if bias:
                            mi = te.matmul(pb[bset + j][:],
                                           ones_row[0:1, 0:128],
                                           brow["o"][0:1,
                                                     j * 512:(j + 1) * 512],
                                           start=False, stop=True)
                        peY.inc(mi)

        @block.vector
        def _(v):
            join(v)
            for i in range(8):
                p = i % 2
                d_x[p].w(v, 16 * (i // 2 + 1))
                for ts in range(4):
                    bset = (ts % 2) * 4
                    gts = 4 * i + ts
                    if gts >= 2:
                        d_y.w(v, 16 * (gts - 1))
                    peY.w(v, 4 * gts + 4)
                    for j in range(4):
                        mi = v.tensor_tensor(
                            yst[ts % 2][:, j * 512:(j + 1) * 512],
                            pb[bset + j][:],
                            xrb[p][:, ts * 2048 + j * 512:
                                  ts * 2048 + (j + 1) * 512],
                            ALU.add)
                    dveY.inc(mi)

        es.close()
        return [d_y, dveY, peY, d_z[0], d_z[1], d_x[0], d_x[1]]

    def emit_ffn1(ffn_i):
        es = ExitStack()
        uidc[0] += 1
        _u = uidc[0]

        def sb(nm_, shape_, dt_):
            return nc.sbuf_tensor(f"{nm_}_{_u}", shape_, dt_)
        xnf = es.enter_context(sb("f1_xnf", [128, 16 * 2048], BF16))
        wgw = [es.enter_context(sb(f"f1_wg{i}", [128, 16 * 512], BF16)) for i in range(2)]
        wuw = [es.enter_context(sb(f"f1_wu{i}", [128, 16 * 512], BF16)) for i in range(2)]
        sg = es.enter_context(sb("f1_sg", [128, 4 * 512], F32))
        hh = es.enter_context(sb("f1_hh", [128, 4 * 512], F32))
        ha = es.enter_context(sb("f1_ha", [128, 4 * 512], F32))
        bcm = es.enter_context(sb("f1_bc", [128, 6 * 512], BF16))
        wn = es.enter_context(sb("f1_wn", [128, 4 * 512], BF16))
        hsp = [es.enter_context(sb(f"f1_hs{i}", [128, 4 * 512], BF16)) for i in range(2)]
        d_xn = mksem("f1_dxn")
        g_w = [mksem("f1_gw0"), mksem("f1_gw1")]
        d_h = [mksem("f1_dh0"), mksem("f1_dh1")]
        peG = mksem("f1_pe")
        acS = mksem("f1_ac")
        dveU = mksem("f1_dvu")
        dveH = mksem("f1_dvh")
        steps = [(hf, w, t4) for hf in range(2) for w in range(16)
                 for t4 in range(4)]

        @block.gpsimd
        def _(g):
            join(g)
            for hf in range(2):
                for w in range(16):
                    wi = hf * 16 + w
                    wp = wi % 2
                    if wi >= 2:
                        peG.w(g, 32 * (wi - 1))
                    for src, dstt in ((wg_in, wgw[wp]), (wu_in, wuw[wp])):
                        g_w[wp].inc(g.dma_start(
                            out=dstt[:].rearrange("p (dk o) -> p dk o", dk=16),
                            in_=src.rearrange("(dk p) o -> p dk o", p=128)
                            [:, :, w * 512:(w + 1) * 512]), 16)

        @block.sync
        def _(sy):
            join(sy)
            for si, (hf, w, t4) in enumerate(steps):
                if w == 0 and t4 == 0:
                    if hf > 0:
                        peG.w(sy, 8 * 64 * hf)
                    for dk in range(16):
                        d_xn.inc(sy.dma_start(
                            out=xnf[:, dk * 2048:(dk + 1) * 2048],
                            in_=xnfm[dk, :, hf * 2048:(hf + 1) * 2048]), 16)
                pt = si % 2
                dveH.w(sy, si + 1)
                d_h[pt].inc(sy.dma_start(
                    out=hscr[w * 4:w * 4 + 4, :,
                             hf * 2048 + t4 * 512:hf * 2048 + (t4 + 1) * 512]
                        .rearrange("c p t -> p c t"),
                    in_=hsp[pt][:].rearrange("p (c t) -> p c t", c=4)), 16)

        @block.tensor
        def _(te):
            join(te)
            for si, (hf, w, t4) in enumerate(steps):
                wi = hf * 16 + w
                wp = wi % 2
                d_xn.w(te, 256 * (hf + 1))
                g_w[wp].w(te, 32 * (wi // 2 + 1))
                if si > 0:
                    acS.w(te, 4 * si)
                for c in range(4):
                    for dk in range(16):
                        mi = te.matmul(
                            pb[c][:],
                            wgw[wp][:, dk * 512 + c: dk * 512 + 512: 4],
                            xnf[:, dk * 2048 + t4 * 512:
                                dk * 2048 + (t4 + 1) * 512],
                            start=(dk == 0), stop=(dk == 15))
                    peG.inc(mi)
                if si > 0:
                    dveU.w(te, 4 * si)
                for c in range(4):
                    for dk in range(16):
                        mi = te.matmul(
                            pb[4 + c][:],
                            wuw[wp][:, dk * 512 + c: dk * 512 + 512: 4],
                            xnf[:, dk * 2048 + t4 * 512:
                                dk * 2048 + (t4 + 1) * 512],
                            start=(dk == 0), stop=(dk == 15))
                    peG.inc(mi)

        @block.scalar
        def _(sc):
            join(sc)
            for si in range(len(steps)):
                for c in range(4):
                    peG.w(sc, 8 * si + c + 1)
                    if si > 0:
                        dveU.w(sc, 4 * (si - 1) + c + 1)
                    acS.inc(sc.activation(sg[:, c * 512:(c + 1) * 512],
                                          pb[c][:], AF.Silu))

        @block.vector
        def _(v):
            join(v)
            C = [slice(c * 512, (c + 1) * 512) for c in range(4)]
            prs = [(0, 1), (0, 2), (0, 3), (1, 2), (1, 3), (2, 3)]
            bi = {pr_: slice(k * 512, (k + 1) * 512)
                  for k, pr_ in enumerate(prs)}
            for si in range(len(steps)):
                pt = si % 2
                for c in range(4):
                    acS.w(v, 4 * si + c + 1)
                    peG.w(v, 8 * si + 4 + c + 1)
                    dveU.inc(v.tensor_tensor(hh[:, C[c]], pb[4 + c][:],
                                             sg[:, C[c]], ALU.mult))
                haI = ha[:].bitcast(mybir.dt.int32)
                hhI = hh[:].bitcast(mybir.dt.int32)
                CI = [slice(c * 512, (c + 1) * 512) for c in range(4)]
                for c in range(4):
                    v.tensor_scalar(haI[:, CI[c]], hhI[:, CI[c]], 0x7fffffff,
                                    None, ALU.bitwise_and)
                for (a_, b_) in prs:
                    v.tensor_tensor(bcm[:, bi[(a_, b_)]], ha[:, C[a_]],
                                    ha[:, C[b_]], ALU.is_ge)
                v.tensor_tensor(wn[:, C[0]], bcm[:, bi[(0, 1)]],
                                bcm[:, bi[(0, 2)]], ALU.add)
                v.tensor_tensor(wn[:, C[0]], wn[:, C[0]], bcm[:, bi[(0, 3)]],
                                ALU.add)
                v.scalar_tensor_tensor(wn[:, C[1]], bcm[:, bi[(0, 1)]], -1.0,
                                       bcm[:, bi[(1, 2)]], ALU.mult, ALU.add)
                v.tensor_tensor(wn[:, C[1]], wn[:, C[1]], bcm[:, bi[(1, 3)]],
                                ALU.add)
                v.scalar_tensor_tensor(wn[:, C[2]], bcm[:, bi[(0, 2)]], -1.0,
                                       bcm[:, bi[(1, 2)]], ALU.mult,
                                       ALU.subtract)
                v.tensor_tensor(wn[:, C[2]], wn[:, C[2]], bcm[:, bi[(2, 3)]],
                                ALU.add)
                v.scalar_tensor_tensor(wn[:, C[3]], bcm[:, bi[(0, 3)]],
                                       -1.0, bcm[:, bi[(1, 3)]], ALU.mult,
                                       ALU.subtract)
                v.tensor_tensor(wn[:, C[3]], wn[:, C[3]], bcm[:, bi[(2, 3)]],
                                ALU.subtract)
                if si >= 2:
                    d_h[pt].w(v, 16 * (si // 2))
                thr = [(2.0, ALU.is_ge), (1.0, ALU.is_ge), (0.0, ALU.is_ge),
                       (-1.0, ALU.is_ge)]
                for c in range(4):
                    t_, op_ = thr[c]
                    mi = v.scalar_tensor_tensor(hsp[pt][:, C[c]], wn[:, C[c]],
                                                t_, hh[:, C[c]], op_, ALU.mult)
                dveH.inc(mi)

        es.close()
        return [d_h[0], d_h[1], dveH, peG, acS, dveU, g_w[0], g_w[1], d_xn]

    def emit_ffn2(ffn_i, dst_ap, d_final):
        es = ExitStack()
        uidc[0] += 1
        _u = uidc[0]

        def sb(nm_, shape_, dt_):
            return nc.sbuf_tensor(f"{nm_}_{_u}", shape_, dt_)
        wdq = es.enter_context(sb("f2_wd", [128, 64 * 512], BF16))
        hbl = [es.enter_context(sb(f"f2_h{i}", [128, 32 * 512], BF16)) for i in range(2)]
        xq = [es.enter_context(sb(f"f2_x{i}", [128, 4 * 512], F32)) for i in range(2)]
        yst = [es.enter_context(sb(f"f2_y{i}", [128, 2 * 512], F32)) for i in range(2)]
        d_wd = mksem("f2_dwd")
        peH2 = mksem("f2_ph")
        d_hb = [mksem("f2_dh0"), mksem("f2_dh1")]
        d_x = [mksem("f2_dx0"), mksem("f2_dx1")]
        peY = mksem("f2_pe")
        dveY = mksem("f2_dv")
        fbase = d_final.n

        @block.sync
        def _(sy):
            join(sy)
            for q in range(4):
                for tb in range(8):
                    g = q * 8 + tb
                    if g >= 1:
                        peH2.w(sy, g)
                    d_hb[0].inc(sy.dma_start(
                        out=hbl[0][:]
                            .rearrange("p (fc t) -> p fc t", fc=32),
                        in_=hscr[0:32, :, tb * 512:(tb + 1) * 512]
                            .rearrange("fc p t -> p fc t")), 16)

        @block.scalar
        def _(sc):
            join(sc)
            sc.wait_ge(g_conv.h, CONV_WDP)
            for q in range(4):
                if q > 0:
                    peY.w(sc, 32 * q)
                for g4 in range(4):
                    d_wd.inc(sc.dma_start(
                        out=wdq[:, g4 * 16 * 512:(g4 + 1) * 16 * 512]
                            .rearrange("p (fc o) -> p fc o", fc=16),
                        in_=wdp[g4 * 16:(g4 + 1) * 16, :,
                                q * 512:(q + 1) * 512]
                            .rearrange("fc p o -> p fc o")), 16)
                for tb in range(8):
                    g = q * 8 + tb
                    if g >= 1:
                        peY.w(sc, 4 * g)
                    d_hb[1].inc(sc.dma_start(
                        out=hbl[1][:]
                            .rearrange("p (fc t) -> p fc t", fc=32),
                        in_=hscr[32:64, :, tb * 512:(tb + 1) * 512]
                            .rearrange("fc p t -> p fc t")), 16)

        @block.gpsimd
        def _(gp):
            join(gp)
            for q in range(4):
                for tb in range(8):
                    g = q * 8 + tb
                    pbuf = g % 2
                    if g >= 2:
                        dveY.w(gp, 4 * (g - 1))
                    d_x[pbuf].inc(gp.dma_start(
                        out=xq[pbuf][:].rearrange("p (ts o) -> p ts o", ts=4),
                        in_=xres[tb * 512:(tb + 1) * 512,
                                 q * 512:(q + 1) * 512]
                            .rearrange("(ts p) o -> p ts o", p=128)), 16)
                    for ts in range(4):
                        dveY.w(gp, 4 * g + ts + 1)
                        d_final.inc(gp.dma_start(
                            out=dst_ap[(tb * 4 + ts) * 128:
                                       (tb * 4 + ts + 1) * 128,
                                       q * 512:(q + 1) * 512],
                            in_=yst[ts % 2][:, (ts // 2) * 512:
                                            (ts // 2 + 1) * 512]), 16)

        @block.tensor
        def _(te):
            join(te)
            for q in range(4):
                te.wait_ge(d_wd.h, d_wd.base + 64 * (q + 1))
                for tb in range(8):
                    g = q * 8 + tb
                    for hhalf in range(2):
                        d_hb[hhalf].w(te, 16 * (g + 1))
                        for fc in range(32):
                            fcg = hhalf * 32 + fc
                            for ts in range(4):
                                bank = (tb % 2) * 4 + ts
                                gts = 4 * g + ts
                                if fc == 0 and hhalf == 0 and gts >= 8:
                                    dveY.w(te, gts - 7)
                                mi = te.matmul(
                                    pb[bank][:],
                                    hbl[hhalf][:, fc * 512 + ts * 128:
                                               fc * 512 + (ts + 1) * 128],
                                    wdq[:, fcg * 512:(fcg + 1) * 512],
                                    start=(fcg == 0), stop=(fcg == 63))
                                if fcg == 63:
                                    peY.inc(mi)
                            if fc == 31 and hhalf == 0:
                                peH2.inc(mi)

        @block.vector
        def _(v):
            join(v)
            for q in range(4):
                for tb in range(8):
                    g = q * 8 + tb
                    pbuf = g % 2
                    d_x[pbuf].w(v, 16 * (g // 2 + 1))
                    for ts in range(4):
                        bank = (tb % 2) * 4 + ts
                        gts = 4 * g + ts
                        peY.w(v, gts + 1)
                        if gts >= 4:
                            v.wait_ge(d_final.h, fbase + 16 * (gts - 3))
                        dveY.inc(v.tensor_tensor(
                            yst[ts % 2][:, (ts // 2) * 512:(ts // 2 + 1) * 512],
                            pb[bank][:],
                            xq[pbuf][:, ts * 512:(ts + 1) * 512],
                            ALU.add))

        es.close()
        return [d_final, dveY, peY, peH2, d_hb[0], d_hb[1], d_x[0], d_x[1],
                d_wd]

    d_out = mksem("d_out")
    d_mid = mksem("d_mid")

    order = [
        ("LN1", lambda: (lambda pg: (lambda fin: (pg[2].close(), fin)[1])(
            emit_ln(x_in, piggy=pg[0], mid_pool=lambda: emit_conv("a"),
                    post_pool=lambda: emit_conv("rest")) + pg[1]))(
            make_wtrans("a"))),
        ("QKV_a", lambda: emit_qkv("a")),
        ("ATT_a", lambda: emit_attn("a")),
        ("OP_a", lambda: emit_oproj("a", x_in)),
        ("LN2", lambda: emit_ln(xres)),
        ("F1P1", lambda: emit_ffn1(0)),
        ("F1P2", lambda: emit_ffn2(0, xres, d_mid)),
        ("LN3", lambda: (lambda pg: (lambda fin: (pg[2].close(), fin)[1])(
            emit_ln(xres, piggy=pg[0]) + pg[1]))(make_wtrans("b"))),
        ("QKV_b", lambda: emit_qkv("b")),
        ("ATT_b", lambda: emit_attn("b")),
        ("OP_b", lambda: emit_oproj("b", xres)),
        ("LN4", lambda: emit_ln(xres)),
        ("F2P1", lambda: emit_ffn1(1)),
        ("F2P2", lambda: emit_ffn2(1, y_out, d_out)),
    ]
    odict = dict(order)
    osel = [(n, odict[n]) for n in order_names] if order_names else order
    for name, fn in osel:
        finals = fn()
        if done(finals):
            break

    if tap is not None:
        tap_map = {
            "xn": (xnfm, [16, 128, S], BF16),
            "x1": (xres, [S, D], F32),
            "q": (qs, [NH, 128, S], BF16),
            "k": (ks, [NH, 128, S], BF16),
            "v": (vs, [NH, NT, 128, 128], BF16),
            "z": (zs, [NH, 128, S], BF16),
            "h": (hscr, [64, 128, S], BF16),
            "wt_qa": (wt["qa"], [D, D], BF16),
            "wdp": (wdp, [64, 128, D], BF16),
        }
        src, shp, dtt = tap_map[tap]
        tout = dt("tapout", shp, dtt, kind="ExternalOutput").ap()
        d_tap = mksem("d_tap")

        @block.sync
        def _(sy):
            join(sy)
            d_tap.inc(sy.dma_start(out=tout, in_=src), 16)
            d_tap.w(sy, 16)

    @block.sync
    def _(sy):
        join(sy)
        if d_out.n:
            d_out.w(sy, d_out.n)

    ges.close()
    return nc


def _shard_inputs(inputs):
    def r2(a):
        return np.ascontiguousarray(np.asarray(a, np.float32).reshape(D, D))

    def c1(a):
        return np.ascontiguousarray(np.asarray(a, np.float32))

    base = {
        "wqa": r2(inputs["L0_attn_a_W_Q"]),
        "wka": r2(inputs["L0_attn_a_W_K"]),
        "wva": r2(inputs["L0_attn_a_W_V"]),
        "woa": r2(inputs["L0_attn_a_W_O"]),
        "wqb": r2(inputs["L0_attn_b_Wq"]),
        "wkb": r2(inputs["L0_attn_b_Wk"]),
        "wvb": r2(inputs["L0_attn_b_Wv"]),
        "wob": r2(inputs["L0_attn_b_Wo"]),
        "bq": c1(inputs["L0_attn_b_bq"]),
        "bk": c1(inputs["L0_attn_b_bk"]),
        "bv": c1(inputs["L0_attn_b_bv"]),
        "bo": c1(inputs["L0_attn_b_bo"]),
        "wg": c1(inputs["L0_ffn_w_gate"]),
        "wu": c1(inputs["L0_ffn_w_up"]),
        "wd": c1(inputs["L0_ffn_w_down"]),
    }
    x = np.asarray(inputs["L0_attn_a_x"], np.float32)
    return [dict(base, x=np.ascontiguousarray(x[i])) for i in range(8)]


def kernel(**inputs):
    in_maps = _shard_inputs(inputs)
    nc = build_nc()
    res = run_bass_kernel_spmd(nc, in_maps, core_ids=list(range(8)))
    return np.stack([np.asarray(r["y"], np.float32) for r in res.results])


# version 14
# speedup vs baseline: 1.1107x; 1.0935x over previous
import math
from contextlib import ExitStack

import numpy as np
import ml_dtypes

import concourse.bass as bass
import concourse.mybir as mybir
from concourse.bass_utils import run_bass_kernel_spmd

F32 = mybir.dt.float32
BF16 = mybir.dt.bfloat16
AF = mybir.ActivationFunctionType
ALU = mybir.AluOpType
AX = mybir.AxisListType
BFNP = ml_dtypes.bfloat16

S = 4096
D = 2048
QKV_STORES = "all"
CONV_PARTS = "all"
LN_DBG = [None]
NH = 16
FI = 8192
NT = S // 128
SC_ATT = 1.0 / math.sqrt(128.0)
ESHIFT = -12.0
EPS = 1e-5


class Sem:
    def __init__(self, nc, es, name):
        self.h = es.enter_context(nc.semaphore(name))
        self.n = 0
        self.base = 0

    def inc(self, instr, k=1):
        instr.then_inc(self.h, k)
        self.n += k
        return instr

    def w(self, eng, v):
        eng.wait_ge(self.h, self.base + v)


def build_nc(upto=99, tap=None, dbg=False, order_names=None):
    nc = bass.Bass("TRN2", target_bir_lowering=False)
    dt = nc.dram_tensor

    x_in = dt("x", [S, D], F32, kind="ExternalInput").ap()
    w_att = {}
    for fl in ("a", "b"):
        for w in ("q", "k", "v", "o"):
            w_att[w + fl] = dt(f"w{w}{fl}", [D, D], F32, kind="ExternalInput").ap()
    b_att = {w: dt(f"b{w}", [D], F32, kind="ExternalInput").ap()
             for w in ("q", "k", "v", "o")}
    wg_in = dt("wg", [D, FI], F32, kind="ExternalInput").ap()
    wu_in = dt("wu", [D, FI], F32, kind="ExternalInput").ap()
    wd_in = dt("wd", [FI, D], F32, kind="ExternalInput").ap()
    y_out = dt("y", [S, D], F32, kind="ExternalOutput").ap()

    xres = dt("xres", [S, D], F32).ap()
    xbf = dt("xbf", [S, D], BF16).ap()
    xnfm = dt("xnfm", [16, 128, S], BF16).ap()
    wc = {k: dt(f"wc_{k}", [D, D], BF16).ap() for k in w_att}
    wt = {k: dt(f"wt_{k}", [D, D], BF16).ap()
          for k in ("qa", "ka", "va", "qb", "kb", "vb")}
    qs = dt("qs", [NH, 128, S], BF16).ap()
    ks = dt("ks", [NH, 128, S], BF16).ap()
    vs = dt("vs", [NH, NT, 128, 128], BF16).ap()
    zs = dt("zs", [NH, 128, S], BF16).ap()
    wdp = dt("wdp", [64, 128, D], BF16).ap()
    hscr = dt("hscr", [64, 128, S], BF16).ap()

    ges = ExitStack()
    block = ges.enter_context(nc.Block())

    semcache = {}

    def mksem(name):
        s = semcache.get(name)
        if s is None:
            s = Sem(nc, ges, name)
            semcache[name] = s
        s.base = s.n
        return s

    prev = [[]]
    uidc = [0]

    def join(eng):
        for s, v in prev[0]:
            if v > 0:
                eng.wait_ge(s.h, v)

    ones_col = ges.enter_context(nc.sbuf_tensor("ones_col", [128, 8], BF16))
    ones_row = ges.enter_context(nc.sbuf_tensor("ones_row", [1, 512], BF16))
    ones_row_f = ges.enter_context(nc.sbuf_tensor("ones_row_f", [1, 128], F32))
    maskbig = ges.enter_context(nc.sbuf_tensor("maskbig", [128, 896], BF16))
    brow = {w: ges.enter_context(nc.sbuf_tensor(f"brow_{w}", [1, D], BF16))
            for w in ("q", "k", "v", "o")}
    pb = [ges.enter_context(nc.psum_tensor(f"pb{i}", [128, 512], F32))
          for i in range(8)]
    cconst = ges.enter_context(nc.sbuf_tensor("cconst", [128, 4], F32))
    iota_es = ExitStack()
    iota_s = iota_es.enter_context(nc.sbuf_tensor("iota_s", [128, 896], F32))

    g_const = mksem("g_const")
    g_conv = mksem("g_conv")

    @block.gpsimd
    def _(g):
        g.memset(ones_col[:], 1.0)
        g.memset(cconst[:, 0:1], EPS)
        g.memset(cconst[:, 1:2], ESHIFT)
        g.memset(ones_row[:], 1.0)
        g.memset(ones_row_f[:], 1.0)
        g_const.inc(g.iota(iota_s[:], pattern=[[1, 896]], base=0,
                           channel_multiplier=-1,
                           allow_small_or_imprecise_dtypes=True), 1)

    def emit_conv(part="all"):
        @block.gpsimd
        def _(g):
            if CONV_PARTS == "none":
                return
            if part in ("a", "all"):
                for k in ("qa", "ka", "va", "oa"):
                    g_conv.inc(g.dma_start(out=wc[k], in_=w_att[k]), 16)
            if part == "a":
                return
            for k in ("qb", "kb", "vb", "ob"):
                g_conv.inc(g.dma_start(out=wc[k], in_=w_att[k]), 16)
            if CONV_PARTS == "wc":
                return
            for w in ("q", "k", "v", "o"):
                g_conv.inc(g.dma_start(
                    out=brow[w][:],
                    in_=b_att[w].rearrange("(a n) -> a n", a=1)), 16)
            if CONV_PARTS == "wcbias":
                return
            for fc in range(64):
                w_, c_ = fc // 4, fc % 4
                g_conv.inc(g.dma_start(
                    out=wdp[fc],
                    in_=wd_in[w_ * 512 + c_: w_ * 512 + c_ + 509: 4, :]), 16)

    @block.vector
    def _(v):
        v.wait_ge(g_const.h, 1)
        g_const.inc(v.tensor_scalar(maskbig[:], iota_s[:], 384.0, None,
                                    ALU.is_ge), 1)

    iota_es.close()

    CONV_A = 4 * 16
    CONV_B = 8 * 16
    CONV_BIAS = 12 * 16
    CONV_WDP = CONV_BIAS + 64 * 16
    nphase = [0]

    def done(finals):
        prev[0] = [(s, s.n) for s in finals]
        nphase[0] += 1
        return nphase[0] >= upto

    def emit_ln(src_ap, piggy=None, mid_pool=None, post_pool=None):
        es = ExitStack()
        uidc[0] += 1
        _u = uidc[0]

        def sb(nm_, shape_, dt_):
            return nc.sbuf_tensor(f"{nm_}_{_u}", shape_, dt_)
        xfm = es.enter_context(sb("ln_xf", [128, 16 * 2048], BF16))
        sqf = es.enter_context(sb("ln_sq", [128, 16 * 1024], BF16))
        rows = es.enter_context(sb("ln_rows", [1, 3 * 1024], F32))
        rowb = es.enter_context(sb("ln_rowb", [1, 2 * 1024], BF16))
        bcst = es.enter_context(sb("ln_bc", [128, 2 * 1024], F32))
        xnd = [es.enter_context(sb(f"ln_xd{i}", [128, 1024], BF16)) for i in range(3)]
        xng = [es.enter_context(sb(f"ln_xg{i}", [128, 1024], BF16)) for i in range(3)]
        t1f = es.enter_context(sb("ln_t1", [128, 1024], F32))
        t1g = es.enter_context(sb("ln_t2", [128, 1024], F32))
        d_c = mksem("ln_dc")
        d_x = mksem("ln_dx")
        d_o = mksem("ln_do")
        a_q = mksem("ln_aq")
        a_r = mksem("ln_arow")
        a_e = mksem("ln_ae")
        peS = mksem("ln_pes")
        peB = mksem("ln_peb")
        d_m = mksem("ln_dm")
        d_n = mksem("ln_dn")
        d_g = mksem("ln_dg")
        NB = 4

        def xsl(b, dk):
            return xfm[:, dk * 2048 + (b % 2) * 1024:
                       dk * 2048 + (b % 2) * 1024 + 1024]

        def emit_evac(sc, b):
            peB.w(sc, b + 1)
            if b >= 1:
                d_n.w(sc, 8 * b)
                d_g.w(sc, 8 * b)
            sc.activation(bcst[:, 0:512], pb[4][:], AF.Copy)
            sc.activation(bcst[:, 512:1024], pb[5][:], AF.Copy)
            sc.activation(bcst[:, 1024:1536], pb[6][:], AF.Copy)
            mi = sc.activation(bcst[:, 1536:2048], pb[7][:], AF.Copy)
            a_e.inc(mi)

        @block.gpsimd
        def _(g):
            join(g)
            for c in range(NB):
                d_c.inc(g.dma_start(out=xbf[c * 1024:(c + 1) * 1024, :],
                                    in_=src_ap[c * 1024:(c + 1) * 1024, :]),
                        16)

        if mid_pool is not None:
            mid_pool()

        @block.gpsimd
        def _(g):
            for b in range(NB):
                a_e.w(g, b + 1)
                for dk in range(8, 16):
                    if b * 16 + dk >= 3:
                        d_o.w(g, 16 * (16 * b + dk - 2))
                    g.tensor_tensor(t1g[:], xsl(b, dk), bcst[:, 0:1024],
                                    ALU.add)
                    d_g.inc(g.tensor_tensor(xng[(dk - 8) % 3][:], t1g[:],
                                            bcst[:, 1024:2048], ALU.mult))

        if post_pool is not None:
            post_pool()

        def emit_stores(sy, b):
            for dk in range(16):
                if dk < 8:
                    d_n.w(sy, 8 * b + dk + 1)
                    srcb = xnd[dk % 3]
                else:
                    d_g.w(sy, 8 * b + (dk - 8) + 1)
                    srcb = xng[(dk - 8) % 3]
                d_o.inc(sy.dma_start(
                    out=xnfm[dk, :, b * 1024:(b + 1) * 1024],
                    in_=srcb[:]), 16)

        @block.sync
        def _(sy):
            join(sy)
            for sbk in range(2):
                d_c.w(sy, 32 * (sbk + 1))
                if sbk > 0:
                    d_n.w(sy, 16)
                    d_g.w(sy, 16)
                for dk in range(16):
                    d_x.inc(sy.dma_start_transpose(
                        out=xfm[:, dk * 2048:(dk + 1) * 2048],
                        in_=xbf[sbk * 2048:(sbk + 1) * 2048,
                                dk * 128:(dk + 1) * 128]), 16)
                for bb in range(2):
                    emit_stores(sy, sbk * 2 + bb)

        @block.scalar
        def _(sc):
            join(sc)
            sc.wait_ge(g_const.h, 1)
            for b in range(NB):
                if b >= 1:
                    emit_evac(sc, b - 1)
                d_x.w(sc, 256 * (b // 2 + 1))
                if b >= 1:
                    peS.w(sc, b)
                for dk in range(16):
                    mi = sc.activation(sqf[:, dk * 1024:(dk + 1) * 1024],
                                       xsl(b, dk), AF.Square)
                a_q.inc(mi)
                d_m.w(sc, 2 * b + 1)
                sc.activation(rows[0:1, 1024:2048], rows[0:1, 2048:3072],
                              AF.Ln, bias=cconst[0:1, 0:1])
                if b >= 1:
                    peB.w(sc, b)
                mi = sc.activation(rowb[0:1, 1024:2048], rows[0:1, 1024:2048],
                                   AF.Exp, scale=-0.5)
                a_r.inc(mi)
                if piggy is not None:
                    piggy(sc, b)
            emit_evac(sc, NB - 1)
            if piggy is not None:
                piggy(sc, NB)

        @block.tensor
        def _(te):
            join(te)
            te.wait_ge(g_const.h, 1)
            for b in range(NB):
                d_x.w(te, 256 * (b // 2 + 1))
                if b >= 1:
                    d_m.w(te, 2 * b)
                for half in range(2):
                    for dk in range(16):
                        mi = te.matmul(pb[0 + half][0:1, :], ones_col[:, 0:1],
                                       xsl(b, dk)[:, half * 512:
                                                  half * 512 + 512],
                                       start=(dk == 0), stop=(dk == 15))
                a_q.w(te, b + 1)
                for half in range(2):
                    for dk in range(16):
                        mi = te.matmul(pb[2 + half][0:1, :], ones_col[:, 0:1],
                                       sqf[:, dk * 1024 + half * 512:
                                           dk * 1024 + half * 512 + 512],
                                       start=(dk == 0), stop=(dk == 15))
                peS.inc(mi)
                d_m.w(te, 2 * b + 2)
                if b >= 1:
                    a_e.w(te, b)
                te.matmul(pb[4][:], ones_row[0:1, 0:128],
                          rowb[0:1, 0:512], start=True, stop=True)
                te.matmul(pb[5][:], ones_row[0:1, 0:128],
                          rowb[0:1, 512:1024], start=True, stop=True)
                a_r.w(te, b + 1)
                te.matmul(pb[6][:], ones_row[0:1, 0:128],
                          rowb[0:1, 1024:1536], start=True, stop=True)
                mi = te.matmul(pb[7][:], ones_row[0:1, 0:128],
                               rowb[0:1, 1536:2048], start=True, stop=True)
                peB.inc(mi)

        @block.vector
        def _(v):
            join(v)
            for b in range(NB):
                peS.w(v, b + 1)
                v.tensor_scalar(rows[0:1, 0:512], pb[0][0:1, :], -1.0 / D,
                                None, ALU.mult)
                v.tensor_scalar(rows[0:1, 512:1024], pb[1][0:1, :], -1.0 / D,
                                None, ALU.mult)
                v.tensor_tensor(rows[0:1, 1024:2048], rows[0:1, 0:1024],
                                rows[0:1, 0:1024], ALU.mult)
                v.tensor_scalar(rows[0:1, 2048:2560], pb[2][0:1, :], 1.0 / D,
                                None, ALU.mult)
                v.tensor_scalar(rows[0:1, 2560:3072], pb[3][0:1, :], 1.0 / D,
                                None, ALU.mult)
                d_m.inc(v.tensor_tensor(rows[0:1, 2048:3072],
                                        rows[0:1, 2048:3072],
                                        rows[0:1, 1024:2048], ALU.subtract))
                if b >= 1:
                    peB.w(v, b)
                d_m.inc(v.tensor_copy(rowb[0:1, 0:1024], rows[0:1, 0:1024]))
                a_e.w(v, b + 1)
                for dk in range(8):
                    if b * 16 + dk >= 3:
                        d_o.w(v, 16 * (16 * b + dk - 2))
                    v.tensor_tensor(t1f[:], xsl(b, dk), bcst[:, 0:1024],
                                    ALU.add)
                    d_n.inc(v.tensor_tensor(xnd[dk % 3][:], t1f[:],
                                            bcst[:, 1024:2048], ALU.mult))

        es.close()
        return [d_o, d_n, d_g, a_e, peB, d_c]

    def make_wtrans(fl):
        es = ExitStack()
        uidc[0] += 1
        _u = uidc[0]
        tb = [es.enter_context(nc.sbuf_tensor(f"wt_tb{i}_{_u}", [128, D], BF16))
              for i in range(12)]
        d_t = mksem("wt_dt")
        d_s = mksem("wt_ds")
        jobs = [(w, dk) for w in ("q", "k", "v") for dk in range(16)]

        def piggy(sy, slot):
            if slot >= 1:
                for j in range((slot - 1) * 12, slot * 12):
                    w, dk = jobs[j]
                    d_t.w(sy, 16 * (j + 1))
                    d_s.inc(sy.dma_start(
                        out=wt[w + fl][dk * 128:(dk + 1) * 128, :],
                        in_=tb[j % 12][:]), 16)
            if slot == 4:
                return
            if slot == 0:
                sy.wait_ge(g_conv.h, CONV_A if fl == "a" else CONV_B)
            for j in range(slot * 12, slot * 12 + 12):
                w, dk = jobs[j]
                if j >= 12:
                    d_s.w(sy, 16 * (j - 11))
                d_t.inc(sy.dma_start_transpose(
                    out=tb[j % 12][:],
                    in_=wc[w + fl][:, dk * 128:(dk + 1) * 128]), 16)

        return piggy, [d_s], es

    def emit_qkv(fl):
        es = ExitStack()
        uidc[0] += 1
        _u = uidc[0]

        def sb(nm_, shape_, dt_):
            return nc.sbuf_tensor(f"{nm_}_{_u}", shape_, dt_)
        xnf = es.enter_context(sb("qk_xnf", [128, 16 * 2048], BF16))
        ws = {w: [es.enter_context(sb(f"qk_w{w}{i}", [128, 16 * 512], BF16))
                  for i in range(2)] for w in ("q", "k", "v")}
        qst = [es.enter_context(sb(f"qk_q{i}", [128, 4 * 512], BF16)) for i in range(2)]
        kst = [es.enter_context(sb(f"qk_k{i}", [128, 4 * 512], BF16)) for i in range(2)]
        vst = [es.enter_context(sb(f"qk_v{i}", [128, 4 * 512], BF16)) for i in range(2)]
        d_xn = mksem("qk_dxn")
        d_w = mksem("qk_dw")
        d_st = [mksem("qk_ds0"), mksem("qk_ds1")]
        peQ = mksem("qk_pe")
        acQ = mksem("qk_ac")
        bias = (fl == "b")
        steps = [(hf, hb, t4) for hf in range(2) for hb in range(4)
                 for t4 in range(4)]

        @block.sync
        def _(sy):
            join(sy)
            for si, (hf, hb, t4) in enumerate(steps):
                p = si % 2
                if hb == 0 and t4 == 0:
                    if hf > 0:
                        peQ.w(sy, 12 * si)
                    for dk in range(16):
                        d_xn.inc(sy.dma_start(
                            out=xnf[:, dk * 2048:(dk + 1) * 2048],
                            in_=xnfm[dk, :, hf * 2048:(hf + 1) * 2048]), 16)
                if t4 == 0:
                    gw = si // 4
                    if gw >= 2:
                        peQ.w(sy, 48 * (gw - 1))
                    for w in ("q", "k", "v"):
                        d_w.inc(sy.dma_start(
                            out=ws[w][gw % 2][:]
                                .rearrange("p (dk o) -> p dk o", dk=16),
                            in_=wt[w + fl].rearrange("(dk p) o -> p dk o",
                                                     p=128)
                            [:, :, hb * 512:(hb + 1) * 512]), 16)
                h0 = hb * 4
                tg = hf * 4 + t4
                acQ.w(sy, 3 * si + 1)
                if QKV_STORES in ("all", "qk"):
                    d_st[p].inc(sy.dma_start(
                        out=qs[h0:h0 + 4, :, tg * 512:(tg + 1) * 512]
                            .rearrange("h p t -> p h t"),
                        in_=qst[p][:].rearrange("p (h t) -> p h t", h=4)), 16)
                acQ.w(sy, 3 * si + 2)
                if QKV_STORES in ("all", "qk"):
                    d_st[p].inc(sy.dma_start(
                        out=ks[h0:h0 + 4, :, tg * 512:(tg + 1) * 512]
                            .rearrange("h p t -> p h t"),
                        in_=kst[p][:].rearrange("p (h t) -> p h t", h=4)), 16)
                acQ.w(sy, 3 * si + 3)
                if QKV_STORES in ("all", "v"):
                    for ts in range(4):
                        d_st[p].inc(sy.dma_start(
                            out=vs[h0:h0 + 4, tg * 4 + ts, :, :]
                                .rearrange("h t dh -> t h dh"),
                            in_=vst[p][:, ts * 512:(ts + 1) * 512]
                                .rearrange("t (h dh) -> t h dh", h=4)), 16)
            if dbg:
                peQ.w(sy, 12 * 32)
                d_st[0].inc(sy.dma_start(out=dbg_xnf, in_=xnf[:]), 16)
                d_st[0].inc(sy.dma_start(out=dbg_wq, in_=ws["q"][:]), 16)
                d_st[0].inc(sy.dma_start(out=dbg_qst, in_=qst[1][:]), 16)

        @block.tensor
        def _(te):
            join(te)
            te.wait_ge(g_const.h, 1)
            if bias:
                te.wait_ge(g_conv.h, CONV_BIAS)
            for si, (hf, hb, t4) in enumerate(steps):
                h0 = hb * 4
                gwp = (si // 4) % 2
                d_xn.w(te, 256 * (hf + 1))
                d_w.w(te, 48 * (si // 4 + 1))
                if si > 0:
                    acQ.w(te, 3 * si)
                for h in range(4):
                    for dk in range(16):
                        mi = te.matmul(pb[h][:],
                                       ws["q"][gwp][:, dk * 512 + h * 128:
                                               dk * 512 + (h + 1) * 128],
                                       xnf[:, dk * 2048 + t4 * 512:
                                           dk * 2048 + (t4 + 1) * 512],
                                       start=(dk == 0),
                                       stop=(dk == 15 and not bias))
                    if bias:
                        mi = te.matmul(pb[h][:],
                                       brow["q"][0:1, (h0 + h) * 128:
                                                 (h0 + h + 1) * 128],
                                       ones_row[0:1, :],
                                       start=False, stop=True)
                    peQ.inc(mi)
                for h in range(4):
                    for dk in range(16):
                        mi = te.matmul(pb[4 + h][:],
                                       ws["k"][gwp][:, dk * 512 + h * 128:
                                               dk * 512 + (h + 1) * 128],
                                       xnf[:, dk * 2048 + t4 * 512:
                                           dk * 2048 + (t4 + 1) * 512],
                                       start=(dk == 0),
                                       stop=(dk == 15 and not bias))
                    if bias:
                        mi = te.matmul(pb[4 + h][:],
                                       brow["k"][0:1, (h0 + h) * 128:
                                                 (h0 + h + 1) * 128],
                                       ones_row[0:1, :],
                                       start=False, stop=True)
                    peQ.inc(mi)
                acQ.w(te, 3 * si + 1)
                for ts in range(4):
                    for dk in range(16):
                        mi = te.matmul(pb[ts][:],
                                       xnf[:, dk * 2048 + t4 * 512 + ts * 128:
                                           dk * 2048 + t4 * 512 + (ts + 1) * 128],
                                       ws["v"][gwp][:, dk * 512:(dk + 1) * 512],
                                       start=(dk == 0),
                                       stop=(dk == 15 and not bias))
                    if bias:
                        mi = te.matmul(pb[ts][:],
                                       ones_row[0:1, 0:128],
                                       brow["v"][0:1, h0 * 128:h0 * 128 + 512],
                                       start=False, stop=True)
                    peQ.inc(mi)

        @block.scalar
        def _(sc):
            join(sc)
            for si in range(32):
                p = si % 2
                if si >= 2:
                    nst = {"all": 6, "qk": 2, "v": 4, "none": 0}[QKV_STORES]
                    if nst:
                        d_st[p].w(sc, 16 * nst * (si // 2))
                peQ.w(sc, 12 * si + 4)
                for h in range(4):
                    mi = sc.activation(qst[p][:, h * 512:(h + 1) * 512],
                                       pb[h][:], AF.Copy)
                acQ.inc(mi)
                peQ.w(sc, 12 * si + 8)
                for h in range(4):
                    mi = sc.activation(kst[p][:, h * 512:(h + 1) * 512],
                                       pb[4 + h][:], AF.Copy)
                acQ.inc(mi)
                peQ.w(sc, 12 * si + 12)
                for ts in range(4):
                    mi = sc.activation(vst[p][:, ts * 512:(ts + 1) * 512],
                                       pb[ts][:], AF.Copy)
                acQ.inc(mi)

        es.close()
        return [d_st[0], d_st[1], peQ, acQ, d_xn, d_w]

    if dbg:
        dbg_xnf = dt("dbg_xnf", [128, 16 * 2048], BF16, kind="ExternalOutput").ap()
        dbg_wq = dt("dbg_wq", [128, 16 * 512], BF16, kind="ExternalOutput").ap()
        dbg_qst = dt("dbg_qst", [128, 4 * 512], BF16, kind="ExternalOutput").ap()

    def emit_attn(fl):
        es = ExitStack()
        uidc[0] += 1
        _u = uidc[0]

        def sb(nm_, shape_, dt_):
            return nc.sbuf_tensor(f"{nm_}_{_u}", shape_, dt_)
        Kb = [es.enter_context(sb(f"at_k{i}", [128, S], BF16)) for i in range(2)]
        Qb = [es.enter_context(sb(f"at_q{i}", [128, S], BF16)) for i in range(2)]
        Vb = [es.enter_context(sb(f"at_v{i}", [128, S], BF16)) for i in range(2)]
        PT = [es.enter_context(sb(f"at_pt{i}", [128, 512], BF16)) for i in range(6)]
        linv = [es.enter_context(sb(f"at_li{i}", [1, 512], F32)) for i in range(2)]
        lsc = es.enter_context(sb("at_lsc", [1, 512], F32))
        invb = [es.enter_context(sb(f"at_iv{i}", [128, 512], F32)) for i in range(2)]
        zst = [es.enter_context(sb(f"at_z{i}", [128, 512], BF16)) for i in range(2)]
        d_kv = [mksem("at_dk0"), mksem("at_dk1")]
        d_z = [mksem("at_dz0"), mksem("at_dz1")]
        peS = mksem("at_peS")
        peA = mksem("at_peA")
        peL = mksem("at_peL")
        peB = mksem("at_peB")
        peH = mksem("at_peH")
        acE = mksem("at_acE")
        acB = mksem("at_acB")
        dveM = mksem("at_dvM")
        dveR = mksem("at_dvR")
        dveZ = mksem("at_dvZ")

        def kbase(h, qb):
            return h * 144 + 2 * qb * qb + 2 * qb

        def emit_loads(sy, h):
            p = h % 2
            d_kv[p].inc(sy.dma_start(out=Kb[p][:], in_=ks[h]), 16)
            d_kv[p].inc(sy.dma_start(out=Qb[p][:], in_=qs[h]), 16)
            for piece in range(4):
                d_kv[p].inc(sy.dma_start(
                    out=Vb[p][:, piece * 1024:(piece + 1) * 1024]
                        .rearrange("t (tb dh) -> t tb dh", tb=8),
                    in_=vs[h, piece * 8:(piece + 1) * 8, :, :]
                        .rearrange("tb t dh -> t tb dh")), 16)

        @block.sync
        def _(sy):
            join(sy)
            emit_loads(sy, 0)
            emit_loads(sy, 1)
            for h in range(NH):
                for qb in range(8):
                    gz = h * 8 + qb
                    pq = gz % 2
                    dveZ.w(sy, gz + 1)
                    d_z[pq].inc(sy.dma_start(
                        out=zs[h][:, qb * 512:(qb + 1) * 512],
                        in_=zst[pq][:]), 16)
                if h + 2 < NH:
                    peH.w(sy, h + 1)
                    emit_loads(sy, h + 2)

        @block.tensor
        def _(te):
            join(te)
            te.wait_ge(g_const.h, 2)

            def pe_bcast(i):
                dveR.w(te, i + 1)
                if i >= 1:
                    acB.w(te, i)
                peB.inc(te.matmul(pb[6][:], ones_row_f[0:1, :],
                                  linv[i % 2][0:1, :], start=True, stop=True))

            for h in range(NH):
                p = h % 2
                d_kv[p].w(te, 96 * (h // 2 + 1))
                for qb in range(8):
                    gz = h * 8 + qb
                    kn = 4 * qb + 4
                    eb = kbase(h, qb)
                    if gz >= 2:
                        dveZ.w(te, gz - 1)
                        dveR.w(te, gz - 1)
                    bursts = []
                    s0 = 0
                    while s0 < kn:
                        bursts.append((s0, min(3, kn - s0)))
                        s0 += 3

                    def consume(c0, c1, kn=kn, qb=qb, gz=gz, eb=eb, p=p):
                        ne_hi = min(c1, 4 * qb - 1)
                        if ne_hi >= c0:
                            acE.w(te, eb + ne_hi + 1)
                        for c in range(c0, c1 + 1):
                            if c >= 4 * qb:
                                dveM.w(te, 4 * gz + (c - 4 * qb) + 1)
                            mi = te.matmul(pb[4 + gz % 2][0:1, :],
                                           ones_col[:, 0:1], PT[c % 6][:],
                                           start=(c == 0), stop=(c == kn - 1))
                            if c == kn - 1:
                                peL.inc(mi)
                            mi = te.matmul(pb[2 + gz % 2][:],
                                           Vb[p][:, c * 128:(c + 1) * 128],
                                           PT[c % 6][:],
                                           start=(c == 0),
                                           stop=(c == kn - 1))
                            if c % 2 == 1 or c == kn - 1:
                                peA.inc(mi)

                    for bi, (s0, sz) in enumerate(bursts):
                        for kb in range(s0, s0 + sz):
                            if kb >= 3:
                                acE.w(te, eb + kb - 2)
                            peS.inc(te.matmul(
                                pb[(0, 1, 7)[kb % 3]][:],
                                Kb[p][:, kb * 128:(kb + 1) * 128],
                                Qb[p][:, qb * 512:(qb + 1) * 512],
                                start=True, stop=True))
                        if gz >= 1 and (s0 == 6 or (kn == 4 and s0 == 3)):
                            pe_bcast(gz - 1)
                        if bi > 0:
                            ps0, psz = bursts[bi - 1]
                            consume(ps0, ps0 + psz - 1)
                    ls0, lsz = bursts[-1]
                    consume(ls0, ls0 + lsz - 1)
                peH.inc(te.nop())
            pe_bcast(NH * 8 - 1)

        @block.scalar
        def _(sc):
            join(sc)
            sc.wait_ge(g_const.h, 1)

            def ac_norm(i):
                peB.w(sc, i + 1)
                if i >= 2:
                    dveZ.w(sc, i - 1)
                acB.inc(sc.activation(invb[i % 2][:], pb[6][:], AF.Copy))

            for h in range(NH):
                for qb in range(8):
                    gz = h * 8 + qb
                    kn = 4 * qb + 4
                    eb = kbase(h, qb)
                    for c in range(kn):
                        peS.w(sc, eb + c + 1)
                        if c >= 6:
                            peA.w(sc, eb // 2 + (c - 6) // 2 + 1)
                        acE.inc(sc.activation(PT[c % 6][:],
                                              pb[(0, 1, 7)[c % 3]][:],
                                              AF.Exp, bias=cconst[:, 1:2],
                                              scale=SC_ATT))
                        if c == min(kn - 1, 7) - 2 and gz >= 1:
                            ac_norm(gz - 1)
            ac_norm(NH * 8 - 1)

        @block.vector
        def _(v):
            join(v)

            def dv_z(i):
                acB.w(v, i + 1)
                pq = i % 2
                if i >= 2:
                    d_z[pq].w(v, 16 * (i // 2))
                dveZ.inc(v.tensor_tensor(zst[pq][:], pb[2 + i % 2][:],
                                         invb[i % 2][:], ALU.mult))

            for h in range(NH):
                for qb in range(8):
                    gz = h * 8 + qb
                    kn = 4 * qb + 4
                    eb = kbase(h, qb)
                    for r in range(4):
                        c = 4 * qb + r
                        acE.w(v, eb + c + 1)
                        dveM.inc(v.tensor_tensor(
                            PT[c % 6][:], PT[c % 6][:],
                            maskbig[:, 384 - 128 * r: 896 - 128 * r],
                            ALU.mult))
                    if gz >= 1:
                        dv_z(gz - 1)
                    peL.w(v, gz + 1)
                    if gz >= 2:
                        peB.w(v, gz - 1)
                    dveR.inc(v.reciprocal(linv[gz % 2][:],
                                          pb[4 + gz % 2][0:1, :]))
            dv_z(NH * 8 - 1)

        es.close()
        return [d_z[0], d_z[1], peH, dveZ, acB, peB, d_kv[0], d_kv[1]]

    def emit_oproj(fl, res_src):
        es = ExitStack()
        uidc[0] += 1
        _u = uidc[0]

        def sb(nm_, shape_, dt_):
            return nc.sbuf_tensor(f"{nm_}_{_u}", shape_, dt_)
        wot = es.enter_context(sb("op_wot", [128, 16 * 2048], BF16))
        zbl = [es.enter_context(sb(f"op_z{i}", [128, 16 * 512], BF16)) for i in range(2)]
        xrb = [es.enter_context(sb(f"op_x{i}", [128, 4 * 2048], F32)) for i in range(2)]
        yst = [es.enter_context(sb(f"op_y{i}", [128, 2048], F32)) for i in range(2)]
        d_wo = mksem("op_dwo")
        d_z = [mksem("op_dz0"), mksem("op_dz1")]
        d_x = [mksem("op_dx0"), mksem("op_dx1")]
        d_y = mksem("op_dy")
        peY = mksem("op_pe")
        dveY = mksem("op_dv")
        bias = (fl == "b")

        @block.sync
        def _(sy):
            join(sy)
            sy.wait_ge(g_conv.h, CONV_A if fl == "a" else CONV_B)
            for dk in range(16):
                d_wo.inc(sy.dma_start_transpose(
                    out=wot[:, dk * 2048:(dk + 1) * 2048],
                    in_=wc["o" + fl][:, dk * 128:(dk + 1) * 128]), 16)
            for i in range(8):
                p = i % 2
                if i >= 2:
                    dveY.w(sy, 4 * (i - 1))
                d_z[p].inc(sy.dma_start(
                    out=zbl[p][:].rearrange("p (h t) -> p h t", h=16),
                    in_=zs[:, :, i * 512:(i + 1) * 512]
                        .rearrange("h p t -> p h t")), 16)
                d_x[p].inc(sy.dma_start(
                    out=xrb[p][:].rearrange("p (ts d2) -> p ts d2", ts=4),
                    in_=res_src[i * 512:(i + 1) * 512, :]
                        .rearrange("(ts p) d2 -> p ts d2", p=128)), 16)
                for ts in range(4):
                    dveY.w(sy, 4 * i + ts + 1)
                    d_y.inc(sy.dma_start(
                        out=xres[(i * 4 + ts) * 128:(i * 4 + ts + 1) * 128, :],
                        in_=yst[ts % 2][:]), 16)

        @block.tensor
        def _(te):
            join(te)
            d_wo.w(te, 256)
            te.wait_ge(g_const.h, 1)
            if bias:
                te.wait_ge(g_conv.h, CONV_BIAS)
            for i in range(8):
                p = i % 2
                d_z[p].w(te, 16 * (i // 2 + 1))
                for ts in range(4):
                    bset = (ts % 2) * 4
                    gts = 4 * i + ts
                    if gts >= 2:
                        dveY.w(te, gts - 1)
                    for j in range(4):
                        for h in range(16):
                            mi = te.matmul(
                                pb[bset + j][:],
                                zbl[p][:, h * 512 + ts * 128:
                                       h * 512 + (ts + 1) * 128],
                                wot[:, h * 2048 + j * 512:
                                    h * 2048 + (j + 1) * 512],
                                start=(h == 0),
                                stop=(h == 15 and not bias))
                        if bias:
                            mi = te.matmul(pb[bset + j][:],
                                           ones_row[0:1, 0:128],
                                           brow["o"][0:1,
                                                     j * 512:(j + 1) * 512],
                                           start=False, stop=True)
                        peY.inc(mi)

        @block.vector
        def _(v):
            join(v)
            for i in range(8):
                p = i % 2
                d_x[p].w(v, 16 * (i // 2 + 1))
                for ts in range(4):
                    bset = (ts % 2) * 4
                    gts = 4 * i + ts
                    if gts >= 2:
                        d_y.w(v, 16 * (gts - 1))
                    peY.w(v, 4 * gts + 4)
                    for j in range(4):
                        mi = v.tensor_tensor(
                            yst[ts % 2][:, j * 512:(j + 1) * 512],
                            pb[bset + j][:],
                            xrb[p][:, ts * 2048 + j * 512:
                                  ts * 2048 + (j + 1) * 512],
                            ALU.add)
                    dveY.inc(mi)

        es.close()
        return [d_y, dveY, peY, d_z[0], d_z[1], d_x[0], d_x[1]]

    def emit_ffn1(ffn_i):
        es = ExitStack()
        uidc[0] += 1
        _u = uidc[0]

        def sb(nm_, shape_, dt_):
            return nc.sbuf_tensor(f"{nm_}_{_u}", shape_, dt_)
        xnf = es.enter_context(sb("f1_xnf", [128, 16 * 2048], BF16))
        wgw = [es.enter_context(sb(f"f1_wg{i}", [128, 16 * 512], BF16)) for i in range(2)]
        wuw = [es.enter_context(sb(f"f1_wu{i}", [128, 16 * 512], BF16)) for i in range(2)]
        sg = es.enter_context(sb("f1_sg", [128, 4 * 512], F32))
        hh = es.enter_context(sb("f1_hh", [128, 4 * 512], F32))
        ha = es.enter_context(sb("f1_ha", [128, 4 * 512], F32))
        bcm = es.enter_context(sb("f1_bc", [128, 6 * 512], BF16))
        wn = es.enter_context(sb("f1_wn", [128, 4 * 512], BF16))
        hsp = [es.enter_context(sb(f"f1_hs{i}", [128, 4 * 512], BF16)) for i in range(2)]
        d_xn = mksem("f1_dxn")
        g_w = [mksem("f1_gw0"), mksem("f1_gw1")]
        d_h = [mksem("f1_dh0"), mksem("f1_dh1")]
        peG = mksem("f1_pe")
        acS = mksem("f1_ac")
        dveU = mksem("f1_dvu")
        dveH = mksem("f1_dvh")
        steps = [(hf, w, t4) for hf in range(2) for w in range(16)
                 for t4 in range(4)]

        @block.gpsimd
        def _(g):
            join(g)
            for hf in range(2):
                for w in range(16):
                    wi = hf * 16 + w
                    wp = wi % 2
                    if wi >= 2:
                        peG.w(g, 32 * (wi - 1))
                    for src, dstt in ((wg_in, wgw[wp]), (wu_in, wuw[wp])):
                        g_w[wp].inc(g.dma_start(
                            out=dstt[:].rearrange("p (dk o) -> p dk o", dk=16),
                            in_=src.rearrange("(dk p) o -> p dk o", p=128)
                            [:, :, w * 512:(w + 1) * 512]), 16)

        @block.sync
        def _(sy):
            join(sy)
            for si, (hf, w, t4) in enumerate(steps):
                if w == 0 and t4 == 0:
                    if hf > 0:
                        peG.w(sy, 8 * 64 * hf)
                    for dk in range(16):
                        d_xn.inc(sy.dma_start(
                            out=xnf[:, dk * 2048:(dk + 1) * 2048],
                            in_=xnfm[dk, :, hf * 2048:(hf + 1) * 2048]), 16)
                pt = si % 2
                dveH.w(sy, si + 1)
                d_h[pt].inc(sy.dma_start(
                    out=hscr[w * 4:w * 4 + 4, :,
                             hf * 2048 + t4 * 512:hf * 2048 + (t4 + 1) * 512]
                        .rearrange("c p t -> p c t"),
                    in_=hsp[pt][:].rearrange("p (c t) -> p c t", c=4)), 16)

        @block.tensor
        def _(te):
            join(te)
            for si, (hf, w, t4) in enumerate(steps):
                wi = hf * 16 + w
                wp = wi % 2
                d_xn.w(te, 256 * (hf + 1))
                g_w[wp].w(te, 32 * (wi // 2 + 1))
                if si > 0:
                    acS.w(te, 4 * si)
                for c in range(4):
                    for dk in range(16):
                        mi = te.matmul(
                            pb[c][:],
                            wgw[wp][:, dk * 512 + c: dk * 512 + 512: 4],
                            xnf[:, dk * 2048 + t4 * 512:
                                dk * 2048 + (t4 + 1) * 512],
                            start=(dk == 0), stop=(dk == 15))
                    peG.inc(mi)
                if si > 0:
                    dveU.w(te, 4 * si)
                for c in range(4):
                    for dk in range(16):
                        mi = te.matmul(
                            pb[4 + c][:],
                            wuw[wp][:, dk * 512 + c: dk * 512 + 512: 4],
                            xnf[:, dk * 2048 + t4 * 512:
                                dk * 2048 + (t4 + 1) * 512],
                            start=(dk == 0), stop=(dk == 15))
                    peG.inc(mi)

        @block.scalar
        def _(sc):
            join(sc)
            for si in range(len(steps)):
                for c in range(4):
                    peG.w(sc, 8 * si + c + 1)
                    if si > 0:
                        dveU.w(sc, 4 * (si - 1) + c + 1)
                    acS.inc(sc.activation(sg[:, c * 512:(c + 1) * 512],
                                          pb[c][:], AF.Silu))

        @block.vector
        def _(v):
            join(v)
            C = [slice(c * 512, (c + 1) * 512) for c in range(4)]
            prs = [(0, 1), (0, 2), (0, 3), (1, 2), (1, 3), (2, 3)]
            bi = {pr_: slice(k * 512, (k + 1) * 512)
                  for k, pr_ in enumerate(prs)}
            for si in range(len(steps)):
                pt = si % 2
                for c in range(4):
                    acS.w(v, 4 * si + c + 1)
                    peG.w(v, 8 * si + 4 + c + 1)
                    dveU.inc(v.tensor_tensor(hh[:, C[c]], pb[4 + c][:],
                                             sg[:, C[c]], ALU.mult))
                haI = ha[:].bitcast(mybir.dt.int32)
                hhI = hh[:].bitcast(mybir.dt.int32)
                CI = [slice(c * 512, (c + 1) * 512) for c in range(4)]
                for c in range(4):
                    v.tensor_scalar(haI[:, CI[c]], hhI[:, CI[c]], 0x7fffffff,
                                    None, ALU.bitwise_and)
                for (a_, b_) in prs:
                    v.tensor_tensor(bcm[:, bi[(a_, b_)]], ha[:, C[a_]],
                                    ha[:, C[b_]], ALU.is_ge)
                v.tensor_tensor(wn[:, C[0]], bcm[:, bi[(0, 1)]],
                                bcm[:, bi[(0, 2)]], ALU.add)
                v.tensor_tensor(wn[:, C[0]], wn[:, C[0]], bcm[:, bi[(0, 3)]],
                                ALU.add)
                v.scalar_tensor_tensor(wn[:, C[1]], bcm[:, bi[(0, 1)]], -1.0,
                                       bcm[:, bi[(1, 2)]], ALU.mult, ALU.add)
                v.tensor_tensor(wn[:, C[1]], wn[:, C[1]], bcm[:, bi[(1, 3)]],
                                ALU.add)
                v.scalar_tensor_tensor(wn[:, C[2]], bcm[:, bi[(0, 2)]], -1.0,
                                       bcm[:, bi[(1, 2)]], ALU.mult,
                                       ALU.subtract)
                v.tensor_tensor(wn[:, C[2]], wn[:, C[2]], bcm[:, bi[(2, 3)]],
                                ALU.add)
                v.scalar_tensor_tensor(wn[:, C[3]], bcm[:, bi[(0, 3)]],
                                       -1.0, bcm[:, bi[(1, 3)]], ALU.mult,
                                       ALU.subtract)
                v.tensor_tensor(wn[:, C[3]], wn[:, C[3]], bcm[:, bi[(2, 3)]],
                                ALU.subtract)
                if si >= 2:
                    d_h[pt].w(v, 16 * (si // 2))
                thr = [(2.0, ALU.is_ge), (1.0, ALU.is_ge), (0.0, ALU.is_ge),
                       (-1.0, ALU.is_ge)]
                for c in range(4):
                    t_, op_ = thr[c]
                    mi = v.scalar_tensor_tensor(hsp[pt][:, C[c]], wn[:, C[c]],
                                                t_, hh[:, C[c]], op_, ALU.mult)
                dveH.inc(mi)

        es.close()
        return [d_h[0], d_h[1], dveH, peG, acS, dveU, g_w[0], g_w[1], d_xn]

    def emit_ffn2(ffn_i, dst_ap, d_final):
        es = ExitStack()
        uidc[0] += 1
        _u = uidc[0]

        def sb(nm_, shape_, dt_):
            return nc.sbuf_tensor(f"{nm_}_{_u}", shape_, dt_)
        wdq = es.enter_context(sb("f2_wd", [128, 64 * 512], BF16))
        hbl = [es.enter_context(sb(f"f2_h{i}", [128, 32 * 512], BF16)) for i in range(2)]
        xq = [es.enter_context(sb(f"f2_x{i}", [128, 4 * 512], F32)) for i in range(2)]
        yst = [es.enter_context(sb(f"f2_y{i}", [128, 2 * 512], F32)) for i in range(2)]
        d_wd = mksem("f2_dwd")
        peH2 = mksem("f2_ph")
        d_hb = [mksem("f2_dh0"), mksem("f2_dh1")]
        d_x = [mksem("f2_dx0"), mksem("f2_dx1")]
        peY = mksem("f2_pe")
        dveY = mksem("f2_dv")
        fbase = d_final.n

        @block.sync
        def _(sy):
            join(sy)
            for q in range(4):
                for tb in range(8):
                    g = q * 8 + tb
                    if g >= 1:
                        peH2.w(sy, g)
                    d_hb[0].inc(sy.dma_start(
                        out=hbl[0][:]
                            .rearrange("p (fc t) -> p fc t", fc=32),
                        in_=hscr[0:32, :, tb * 512:(tb + 1) * 512]
                            .rearrange("fc p t -> p fc t")), 16)

        @block.scalar
        def _(sc):
            join(sc)
            sc.wait_ge(g_conv.h, CONV_WDP)
            for q in range(4):
                if q > 0:
                    peY.w(sc, 32 * q)
                for g4 in range(4):
                    d_wd.inc(sc.dma_start(
                        out=wdq[:, g4 * 16 * 512:(g4 + 1) * 16 * 512]
                            .rearrange("p (fc o) -> p fc o", fc=16),
                        in_=wdp[g4 * 16:(g4 + 1) * 16, :,
                                q * 512:(q + 1) * 512]
                            .rearrange("fc p o -> p fc o")), 16)
                for tb in range(8):
                    g = q * 8 + tb
                    if g >= 1:
                        peY.w(sc, 4 * g)
                    d_hb[1].inc(sc.dma_start(
                        out=hbl[1][:]
                            .rearrange("p (fc t) -> p fc t", fc=32),
                        in_=hscr[32:64, :, tb * 512:(tb + 1) * 512]
                            .rearrange("fc p t -> p fc t")), 16)

        @block.gpsimd
        def _(gp):
            join(gp)
            for q in range(4):
                for tb in range(8):
                    g = q * 8 + tb
                    pbuf = g % 2
                    if g >= 2:
                        dveY.w(gp, 4 * (g - 1))
                    d_x[pbuf].inc(gp.dma_start(
                        out=xq[pbuf][:].rearrange("p (ts o) -> p ts o", ts=4),
                        in_=xres[tb * 512:(tb + 1) * 512,
                                 q * 512:(q + 1) * 512]
                            .rearrange("(ts p) o -> p ts o", p=128)), 16)
                    for ts in range(4):
                        dveY.w(gp, 4 * g + ts + 1)
                        d_final.inc(gp.dma_start(
                            out=dst_ap[(tb * 4 + ts) * 128:
                                       (tb * 4 + ts + 1) * 128,
                                       q * 512:(q + 1) * 512],
                            in_=yst[ts % 2][:, (ts // 2) * 512:
                                            (ts // 2 + 1) * 512]), 16)

        @block.tensor
        def _(te):
            join(te)
            for q in range(4):
                te.wait_ge(d_wd.h, d_wd.base + 64 * (q + 1))
                for tb in range(8):
                    g = q * 8 + tb
                    for hhalf in range(2):
                        d_hb[hhalf].w(te, 16 * (g + 1))
                        for fc in range(32):
                            fcg = hhalf * 32 + fc
                            for ts in range(4):
                                bank = (tb % 2) * 4 + ts
                                gts = 4 * g + ts
                                if fc == 0 and hhalf == 0 and gts >= 8:
                                    dveY.w(te, gts - 7)
                                mi = te.matmul(
                                    pb[bank][:],
                                    hbl[hhalf][:, fc * 512 + ts * 128:
                                               fc * 512 + (ts + 1) * 128],
                                    wdq[:, fcg * 512:(fcg + 1) * 512],
                                    start=(fcg == 0), stop=(fcg == 63))
                                if fcg == 63:
                                    peY.inc(mi)
                            if fc == 31 and hhalf == 0:
                                peH2.inc(mi)

        @block.vector
        def _(v):
            join(v)
            for q in range(4):
                for tb in range(8):
                    g = q * 8 + tb
                    pbuf = g % 2
                    d_x[pbuf].w(v, 16 * (g // 2 + 1))
                    for ts in range(4):
                        bank = (tb % 2) * 4 + ts
                        gts = 4 * g + ts
                        peY.w(v, gts + 1)
                        if gts >= 4:
                            v.wait_ge(d_final.h, fbase + 16 * (gts - 3))
                        dveY.inc(v.tensor_tensor(
                            yst[ts % 2][:, (ts // 2) * 512:(ts // 2 + 1) * 512],
                            pb[bank][:],
                            xq[pbuf][:, ts * 512:(ts + 1) * 512],
                            ALU.add))

        es.close()
        return [d_final, dveY, peY, peH2, d_hb[0], d_hb[1], d_x[0], d_x[1],
                d_wd]

    d_out = mksem("d_out")
    d_mid = mksem("d_mid")

    order = [
        ("LN1", lambda: (lambda pg: (lambda fin: (pg[2].close(), fin)[1])(
            emit_ln(x_in, piggy=pg[0], mid_pool=lambda: emit_conv("a"),
                    post_pool=lambda: emit_conv("rest")) + pg[1]))(
            make_wtrans("a"))),
        ("QKV_a", lambda: emit_qkv("a")),
        ("ATT_a", lambda: emit_attn("a")),
        ("OP_a", lambda: emit_oproj("a", x_in)),
        ("LN2", lambda: emit_ln(xres)),
        ("F1P1", lambda: emit_ffn1(0)),
        ("F1P2", lambda: emit_ffn2(0, xres, d_mid)),
        ("LN3", lambda: (lambda pg: (lambda fin: (pg[2].close(), fin)[1])(
            emit_ln(xres, piggy=pg[0]) + pg[1]))(make_wtrans("b"))),
        ("QKV_b", lambda: emit_qkv("b")),
        ("ATT_b", lambda: emit_attn("b")),
        ("OP_b", lambda: emit_oproj("b", xres)),
        ("LN4", lambda: emit_ln(xres)),
        ("F2P1", lambda: emit_ffn1(1)),
        ("F2P2", lambda: emit_ffn2(1, y_out, d_out)),
    ]
    odict = dict(order)
    osel = [(n, odict[n]) for n in order_names] if order_names else order
    for name, fn in osel:
        finals = fn()
        if done(finals):
            break

    if tap is not None:
        tap_map = {
            "xn": (xnfm, [16, 128, S], BF16),
            "x1": (xres, [S, D], F32),
            "q": (qs, [NH, 128, S], BF16),
            "k": (ks, [NH, 128, S], BF16),
            "v": (vs, [NH, NT, 128, 128], BF16),
            "z": (zs, [NH, 128, S], BF16),
            "h": (hscr, [64, 128, S], BF16),
            "wt_qa": (wt["qa"], [D, D], BF16),
            "wdp": (wdp, [64, 128, D], BF16),
        }
        src, shp, dtt = tap_map[tap]
        tout = dt("tapout", shp, dtt, kind="ExternalOutput").ap()
        d_tap = mksem("d_tap")

        @block.sync
        def _(sy):
            join(sy)
            d_tap.inc(sy.dma_start(out=tout, in_=src), 16)
            d_tap.w(sy, 16)

    @block.sync
    def _(sy):
        join(sy)
        if d_out.n:
            d_out.w(sy, d_out.n)

    ges.close()
    return nc


def _shard_inputs(inputs):
    def r2(a):
        return np.ascontiguousarray(np.asarray(a, np.float32).reshape(D, D))

    def c1(a):
        return np.ascontiguousarray(np.asarray(a, np.float32))

    base = {
        "wqa": r2(inputs["L0_attn_a_W_Q"]),
        "wka": r2(inputs["L0_attn_a_W_K"]),
        "wva": r2(inputs["L0_attn_a_W_V"]),
        "woa": r2(inputs["L0_attn_a_W_O"]),
        "wqb": r2(inputs["L0_attn_b_Wq"]),
        "wkb": r2(inputs["L0_attn_b_Wk"]),
        "wvb": r2(inputs["L0_attn_b_Wv"]),
        "wob": r2(inputs["L0_attn_b_Wo"]),
        "bq": c1(inputs["L0_attn_b_bq"]),
        "bk": c1(inputs["L0_attn_b_bk"]),
        "bv": c1(inputs["L0_attn_b_bv"]),
        "bo": c1(inputs["L0_attn_b_bo"]),
        "wg": c1(inputs["L0_ffn_w_gate"]),
        "wu": c1(inputs["L0_ffn_w_up"]),
        "wd": c1(inputs["L0_ffn_w_down"]),
    }
    x = np.asarray(inputs["L0_attn_a_x"], np.float32)
    return [dict(base, x=np.ascontiguousarray(x[i])) for i in range(8)]


def kernel(**inputs):
    in_maps = _shard_inputs(inputs)
    nc = build_nc()
    res = run_bass_kernel_spmd(nc, in_maps, core_ids=list(range(8)))
    return np.stack([np.asarray(r["y"], np.float32) for r in res.results])
